# version 01
# baseline (speedup 1.0000x reference)
import math
import numpy as np
import ml_dtypes
import concourse.bass as bass
import concourse.mybir as mybir
from concourse.bass_utils import run_bass_kernel_spmd

F32 = mybir.dt.float32
BF16 = mybir.dt.bfloat16
I32 = mybir.dt.int32
AF = mybir.ActivationFunctionType
ALU = mybir.AluOpType
AX = mybir.AxisListType

B, T, D = 8, 2048, 1024
NT = B * T
NTILE = 16
TT = NT // NTILE
DC = D // 128
DH = 4096
HC = DH // 128
EPS = 1e-5
QP = 127.0
MINR = 1e-6

SBUF_LO = 16512
SBUF_HI = 229344


class Arena:
    def __init__(self, nc):
        self.nc = nc
        self.free = [(SBUF_LO, SBUF_HI)]
        self.live = {}

    def alloc(self, name, shape, dtype, align=128):
        esz = mybir.dt.size(dtype)
        need = int(np.prod(shape[1:])) * esz
        big = need >= 8192
        order = range(len(self.free) - 1, -1, -1) if big else range(len(self.free))
        for i in order:
            lo, hi = self.free[i]
            if big:
                start = (hi - need) // align * align
                if start < lo:
                    continue
            else:
                start = (lo + align - 1) // align * align
                if start + need > hi:
                    continue
            self.free.pop(i)
            if start > lo:
                self.free.insert(i, (lo, start))
                if start + need < hi:
                    self.free.insert(i + 1, (start + need, hi))
            elif start + need < hi:
                self.free.insert(i, (start + need, hi))
            h = self.nc.alloc_sbuf_tensor_at(name, list(shape), dtype, offset=start)
            self.live[name] = (start, start + need)
            return h.ap()
        raise MemoryError(f"SBUF OOM: {name} {shape} {dtype} free={self.free}")

    def freebuf(self, *names):
        for name in names:
            lo, hi = self.live.pop(name)
            self.free.append((lo, hi))
        self.free.sort()
        merged = []
        for lo, hi in self.free:
            if merged and lo <= merged[-1][1]:
                merged[-1] = (merged[-1][0], max(hi, merged[-1][1]))
            else:
                merged.append((lo, hi))
        self.free = merged


class Sem:
    def __init__(self, nc, name):
        self.h = nc.alloc_semaphore(name)
        self.v = 0

    def inc(self, n=1):
        self.v += n
        return self.v


def f32(x):
    return np.ascontiguousarray(np.asarray(x, np.float32))


def prep_inputs(inputs, core):
    d = {}
    x = f32(inputs["L0_attn_x"])
    xr = np.concatenate([x[core:], x[:core]], axis=0)
    xT = xr.reshape(NT, D).T
    d["xT_t"] = np.ascontiguousarray(
        xT.reshape(DC, 128, NTILE, TT).transpose(2, 1, 0, 3)).reshape(NTILE, 128, DC * TT)

    def lhsT_tiles(W_in_out, width):
        return np.ascontiguousarray(
            W_in_out.reshape(DC, 128, width).transpose(1, 0, 2)).reshape(128, DC * width)

    d["WqT"] = lhsT_tiles(f32(inputs["L0_attn_Wq"]).T, D)
    d["WkT"] = lhsT_tiles(f32(inputs["L0_attn_Wk"]).T, D)
    d["WvT"] = lhsT_tiles(f32(inputs["L0_attn_Wv"]).T, D)
    d["bqkv"] = np.ascontiguousarray(np.stack(
        [f32(inputs["L0_attn_bq"]).reshape(DC, 128).T,
         f32(inputs["L0_attn_bk"]).reshape(DC, 128).T,
         f32(inputs["L0_attn_bv"]).reshape(DC, 128).T], axis=2).reshape(128, DC * 3))
    pw = f32(inputs["L0_attn_weights"]).transpose(2, 0, 1).reshape(64, 16 * 64)
    d["phiWT"] = np.ascontiguousarray(np.concatenate([pw, pw], axis=0))
    d["w1T_t"] = lhsT_tiles(f32(inputs["L0_ffn_w1"]).T, DH)
    d["w2T_t"] = np.ascontiguousarray(
        f32(inputs["L0_ffn_w2"]).T.reshape(HC, 128, D).transpose(1, 0, 2)).reshape(128, HC * D)
    d["b1"] = np.ascontiguousarray(f32(inputs["L0_ffn_b1"]).reshape(HC, 128).T)
    d["b2"] = np.ascontiguousarray(f32(inputs["L0_ffn_b2"]).reshape(DC, 128).T)
    d["Wkvd_t"] = lhsT_tiles(f32(inputs["L1_attn_W_kv_down"]), 128)
    d["Wku"] = f32(inputs["L1_attn_W_k_up"])
    d["Wvu"] = f32(inputs["L1_attn_W_v_up"])
    d["Wq1_t"] = lhsT_tiles(f32(inputs["L1_attn_W_q"]), D)
    d["Wqr_t"] = lhsT_tiles(f32(inputs["L1_attn_W_q_rope"]), 32)
    d["Wkr"] = f32(inputs["L1_attn_W_k_rope"])
    d["Wo_t"] = lhsT_tiles(f32(inputs["L1_attn_W_o"]), D)

    def l1_slabs(W_out_in):
        WT = f32(W_out_in).T
        ic = WT.shape[0] // 128
        return np.ascontiguousarray(
            WT.reshape(ic, 128, HC, 128).transpose(2, 1, 0, 3)).reshape(HC, 128, ic * 128)

    d["l1w1_t"] = l1_slabs(inputs["L1_ffn_w1_w"])
    d["l1w2_t"] = l1_slabs(inputs["L1_ffn_w2_w"])
    d["l1h1_t"] = l1_slabs(inputs["L1_ffn_h1_w"])
    d["l1h2_t"] = l1_slabs(inputs["L1_ffn_h2_w"])
    d["l1w3_t"] = np.ascontiguousarray(f32(inputs["L1_ffn_w3_w"]).T.reshape(HC, 128, D))
    d["l1scales"] = np.ascontiguousarray(np.concatenate(
        [f32(inputs["L1_ffn_w1_s"]).reshape(HC, 128).T,
         f32(inputs["L1_ffn_w2_s"]).reshape(HC, 128).T,
         f32(inputs["L1_ffn_h1_s"]).reshape(HC, 128).T,
         f32(inputs["L1_ffn_h2_s"]).reshape(HC, 128).T,
         f32(inputs["L1_ffn_w3_s"]).reshape(128, DC)], axis=1))
    return d


def build(phases=99, dbg=()):
    nc = bass.Bass("TRN2", target_bir_lowering=False)
    ar = Arena(nc)

    def din(name, shape, dt=F32):
        return nc.dram_tensor(name, list(shape), dt, kind="ExternalInput").ap()

    xT_t = din("xT_t", [NTILE, 128, DC * TT])
    WqT = din("WqT", [128, DC * D]); WkT = din("WkT", [128, DC * D]); WvT = din("WvT", [128, DC * D])
    bqkv = din("bqkv", [128, DC * 3])
    phiWT = din("phiWT", [128, 1024])
    w1T_t = din("w1T_t", [128, DC * DH])
    w2T_t = din("w2T_t", [128, HC * D])
    b1_in = din("b1", [128, HC]); b2_in = din("b2", [128, DC])
    Wkvd_t = din("Wkvd_t", [128, DC * 128])
    Wku_in = din("Wku", [128, D]); Wvu_in = din("Wvu", [128, D])
    Wq1_t = din("Wq1_t", [128, DC * D]); Wqr_t = din("Wqr_t", [128, DC * 32])
    Wkr_in = din("Wkr", [128, 32]); Wo_t = din("Wo_t", [128, DC * D])
    l1w1_t = din("l1w1_t", [HC, 128, DC * 128]); l1w2_t = din("l1w2_t", [HC, 128, DC * 128])
    l1h1_t = din("l1h1_t", [HC, 128, HC * 128]); l1h2_t = din("l1h2_t", [HC, 128, HC * 128])
    l1w3_t = din("l1w3_t", [HC, 128, D])
    l1scales = din("l1scales", [128, HC * 4 + DC])

    y = nc.dram_tensor("y", [T, D], F32, kind="ExternalOutput").ap()

    v_hbm = nc.dram_tensor("v_hbm", [DC, 128, T], F32, kind="Internal").ap()
    h_own = nc.dram_tensor("h_own", [HC, 128, T], F32, kind="Internal").ap()
    x1p_hbm = nc.dram_tensor("x1p_hbm", [DC, 128, T], F32, kind="Internal").ap()
    x1_hbm = nc.dram_tensor("x1_hbm", [DC, 128, T], F32, kind="Internal").ap()
    x1tok_hbm = nc.dram_tensor("x1tok_hbm", [T // 128, 128, D], F32, kind="Internal").ap()
    g1_hbm = nc.dram_tensor("g1_hbm", [HC, 128, T], BF16, kind="Internal").ap()

    ident_f_h = nc.inline_tensor(np.eye(128, dtype=np.float32), name="c_ident_f")
    ident_b_h = nc.inline_tensor(np.eye(128).astype(ml_dtypes.bfloat16), name="c_ident_b")
    ones_col_h = nc.inline_tensor(np.ones((128, 1), np.float32), name="c_ones_col")
    ones_row_h = nc.inline_tensor(np.ones((1, 128), np.float32), name="c_ones_row")
    maskv = np.where(np.triu(np.ones((128, 128), bool), 1), np.float32(-3e38), np.float32(0))
    mask_h = nc.inline_tensor(np.ascontiguousarray(maskv), name="c_mask")
    invf = 1.0 / 10000.0 ** (np.arange(0, 32, 2, dtype=np.float32) / 32)
    fr = np.outer(np.arange(T, dtype=np.float32), invf)
    selv = np.zeros((2, 128), np.float32)
    selv[0, 0:64] = 1.0
    selv[1, 64:128] = 1.0
    sel_h = nc.inline_tensor(np.ascontiguousarray(selv), name="c_sel")
    cosv = np.cos(fr).T.astype(np.float32)
    sinv = np.sin(fr).T.astype(np.float32)
    cosT_h = nc.inline_tensor(np.ascontiguousarray(np.concatenate([cosv, cosv], 0)),
                              name="c_cosT")
    sinTs_h = nc.inline_tensor(np.ascontiguousarray(np.concatenate([-sinv, sinv], 0)),
                               name="c_sinTs")
    invsq96_h = nc.inline_tensor(np.full((128, 1), 1.0 / math.sqrt(96.0), np.float32),
                                 name="c_invsq96")

    dbg_out = {}

    def dbg_tensor(name, shape, dt=F32):
        if name in dbg:
            dbg_out[name] = nc.dram_tensor("dbg_" + name, list(shape), dt,
                                           kind="ExternalOutput").ap()
        return dbg_out.get(name)

    ident_f = ar.alloc("ident_f", [128, 128], F32)
    ident_b = ar.alloc("ident_b", [128, 128], BF16)
    ones_col = ar.alloc("ones_col", [128, 1], F32)
    ones_row = ar.alloc("ones_row", [1, 128], F32)
    mask_sb = ar.alloc("mask_sb", [128, 128], F32)

    bqkv_sb = ar.alloc("bqkv_sb", [128, DC * 3], F32)
    b1_sb = ar.alloc("b1_sb", [128, HC], F32)
    b2_sb = ar.alloc("b2_sb", [128, DC], F32)
    l1s_sb = ar.alloc("l1s_sb", [128, HC * 4 + DC], F32)
    phiW_sb = ar.alloc("phiW_sb", [128, 1024], BF16)
    strip = ar.alloc("strip", [128, 64], F32)
    srow = ar.alloc("srow", [1, 2048], F32)
    scl = ar.alloc("scl", [1, 64], F32)
    sclB = ar.alloc("sclB", [128, 16], F32)
    stats_rp = ar.alloc("stats_rp", [128, 256], F32)
    mxg = ar.alloc("mxg", [128, 128], F32)
    sel_sb = ar.alloc("sel_sb", [2, 128], F32)
    b2c = ar.alloc("b2c", [128, DC], F32)
    stats_rp2 = ar.alloc("stats_rp2", [128, 32], F32)
    b1c = ar.alloc("b1c", [128, HC], F32)

    psA = nc.alloc_psum_tensor("psA", [128, 2048], F32).ap()
    psB = nc.alloc_psum_tensor("psB", [128, 2048], F32).ap()

    sIN = Sem(nc, "sIN"); sWT = Sem(nc, "sWT"); sPE = Sem(nc, "sPE")
    sVE = Sem(nc, "sVE"); sSE = Sem(nc, "sSE"); sDO = Sem(nc, "sDO")
    sGC = Sem(nc, "sGC")

    state = {}

    with nc.Block() as block:
        def p0_sync(sync):
            for dst, src in ((ident_f, ident_f_h.ap()), (ident_b, ident_b_h.ap()),
                             (ones_col, ones_col_h.ap()), (ones_row, ones_row_h.ap()),
                             (mask_sb, mask_h.ap()),
                             (sel_sb, sel_h.ap()),
                             (bqkv_sb, bqkv), (b1_sb, b1_in), (b2_sb, b2_in),
                             (l1s_sb, l1scales)):
                sync.dma_start(out=dst[:], in_=src[:]).then_inc(sIN.h, 16)
                sIN.inc(16)
        block.sync(p0_sync)
        P0_CONST = sIN.v

        def p0_gp(g):
            g.dma_start(out=phiW_sb[:], in_=phiWT[:]).then_inc(sWT.h, 16)
            sWT.inc(16)
        block.gpsimd(p0_gp)

        xr = [ar.alloc("xr0", [128, DC * TT], F32), ar.alloc("xr1", [128, DC * TT], F32)]
        sqf = ar.alloc("sqf", [128, DC * TT], F32)
        rB = [ar.alloc("rB0", [128, TT], F32), ar.alloc("rB1", [128, TT], F32)]
        mB = [ar.alloc("mB0", [128, TT], F32), ar.alloc("mB1", [128, TT], F32)]
        xn_own = ar.alloc("xn_own", [128, DC * T], F32)
        rp = ar.alloc("rp", [128, 64], F32)

        x_ld = [None] * NTILE
        sq_rd = {}
        st_mm = [None] * NTILE
        st_ev = [None] * NTILE
        st_rp1 = [None] * NTILE
        st_ms = [None] * NTILE
        st_sq = [None] * NTILE
        st_m2 = [None] * NTILE
        st_rp2 = [None] * NTILE
        st_bc = [None] * NTILE
        st_cp = [None] * NTILE
        ap_done = [None] * NTILE

        def p1_load(sync, i):
            if i == 0:
                sync.wait_ge(sIN.h, P0_CONST)
            if i >= 2:
                sync.wait_ge(sVE.h, ap_done[i - 2])
            sync.dma_start(out=xr[i % 2][:], in_=xT_t[i]).then_inc(sIN.h, 16)
            x_ld[i] = sIN.inc(16)

        def p1_sq(s, i):
            s.wait_ge(sIN.h, x_ld[i])
            if i >= 1:
                s.wait_ge(sPE.h, st_mm[i - 1])
            for fc in range(DC):
                s.activation(out=sqf[:, fc * TT:(fc + 1) * TT],
                             in_=xr[i % 2][:, fc * TT:(fc + 1) * TT],
                             func=AF.Square).then_inc(sSE.h, 1)
                sq_rd[(i, fc)] = sSE.inc()

        def p1_stats_mm(t, i):
            t.wait_ge(sIN.h, x_ld[i])
            if i >= 1:
                t.wait_ge(sVE.h, st_ev[i - 1])
            for fc in range(DC):
                st = fc == 0
                sp = fc == DC - 1
                for nn_ in range(2):
                    t.matmul(out=psB[0:1, nn_ * 512:(nn_ + 1) * 512], lhsT=ones_col[:, 0:1],
                             rhs=xr[i % 2][:, fc * TT + nn_ * 512: fc * TT + (nn_ + 1) * 512],
                             start=st, stop=sp)
                t.wait_ge(sSE.h, sq_rd[(i, fc)])
                for nn_ in range(2):
                    ins = t.matmul(out=psB[0:1, 1024 + nn_ * 512: 1024 + (nn_ + 1) * 512],
                                   lhsT=ones_col[:, 0:1],
                                   rhs=sqf[:, fc * TT + nn_ * 512: fc * TT + (nn_ + 1) * 512],
                                   start=st, stop=sp)
            ins.then_inc(sPE.h, 1)
            st_mm[i] = sPE.inc()

        def p1_stats_ev(v, i):
            v.wait_ge(sPE.h, st_mm[i])
            if i >= 1:
                v.wait_ge(sGC.h, st_rp2[i - 1])
            v.tensor_copy(out=srow[0:1, 0:TT], in_=psB[0:1, 0:TT])
            v.tensor_copy(out=srow[0:1, TT:2 * TT], in_=psB[0:1, TT:2 * TT]).then_inc(sVE.h, 1)
            st_ev[i] = sVE.inc()

        def p1_rp_in(g, i):
            g.wait_ge(sVE.h, st_ev[i])
            g.dma_start(out=rp[:, 0:8], in_=srow[0:1, 0:TT]).then_inc(sGC.h, 16)
            g.dma_start(out=rp[:, 8:16], in_=srow[0:1, TT:2 * TT]).then_inc(sGC.h, 16)
            st_rp1[i] = sGC.inc(32)

        def p1_strip1(v, i):
            v.wait_ge(sGC.h, st_rp1[i])
            v.tensor_scalar(out=rp[:, 16:24], in0=rp[:, 0:8], scalar1=-1.0 / D,
                            scalar2=None, op0=ALU.mult)
            v.tensor_scalar(out=rp[:, 8:16], in0=rp[:, 8:16], scalar1=1.0 / D,
                            scalar2=None, op0=ALU.mult)
            v.drain()
            v.tensor_tensor(out=rp[:, 24:32], in0=rp[:, 16:24], in1=rp[:, 16:24], op=ALU.mult)
            v.drain()
            v.tensor_tensor(out=rp[:, 32:40], in0=rp[:, 8:16], in1=rp[:, 24:32], op=ALU.subtract)
            v.drain()
            v.tensor_scalar(out=rp[:, 8:16], in0=rp[:, 32:40], scalar1=EPS, scalar2=None,
                            op0=ALU.add).then_inc(sVE.h, 1)
            st_ms[i] = sVE.inc()

        def p1_sqrt(s, i):
            s.wait_ge(sVE.h, st_ms[i])
            s.sqrt(out=rp[:, 56:64], in_=rp[:, 8:16]).then_inc(sSE.h, 1)
            st_sq[i] = sSE.inc()

        def p1_strip2(v, i):
            v.wait_ge(sSE.h, st_sq[i])
            v.reciprocal(out=rp[:, 40:48], in_=rp[:, 56:64])
            v.drain()
            v.tensor_tensor(out=rp[:, 24:32], in0=rp[:, 16:24], in1=rp[:, 40:48], op=ALU.mult)
            v.drain()
            v.tensor_copy(out=stats_rp[:, i * 8:(i + 1) * 8], in_=rp[:, 40:48])
            v.tensor_copy(out=stats_rp[:, 128 + i * 8:128 + (i + 1) * 8],
                          in_=rp[:, 24:32]).then_inc(sVE.h, 1)
            st_m2[i] = sVE.inc()

        def p1_rp_out(g, i):
            g.wait_ge(sVE.h, st_m2[i])
            g.dma_start(out=srow[0:1, 0:TT], in_=rp[:, 40:48]).then_inc(sGC.h, 16)
            g.dma_start(out=srow[0:1, TT:2 * TT], in_=rp[:, 24:32]).then_inc(sGC.h, 16)
            st_rp2[i] = sGC.inc(32)

        def p1_bcast(t, i):
            t.wait_ge(sGC.h, st_rp2[i])
            if i >= 1:
                t.wait_ge(sVE.h, st_cp[i - 1])
            for nn_ in range(2):
                t.matmul(out=psA[:, nn_ * 512:(nn_ + 1) * 512],
                         lhsT=ones_row[0:1, :], rhs=srow[0:1, nn_ * 512:(nn_ + 1) * 512],
                         start=True, stop=True)
            for nn_ in range(2):
                ins = t.matmul(out=psA[:, 1024 + nn_ * 512: 1024 + (nn_ + 1) * 512],
                               lhsT=ones_row[0:1, :],
                               rhs=srow[0:1, TT + nn_ * 512: TT + (nn_ + 1) * 512],
                               start=True, stop=True)
            ins.then_inc(sPE.h, 1)
            st_bc[i] = sPE.inc()

        def p1_bc_cp(v, i):
            v.wait_ge(sPE.h, st_bc[i])
            v.tensor_copy(out=rB[i % 2][:], in_=psA[:, 0:TT])
            v.tensor_copy(out=mB[i % 2][:], in_=psA[:, TT:2 * TT]).then_inc(sVE.h, 1)
            st_cp[i] = sVE.inc()

        def p1_apply(v, i):
            own = i < 2
            xb_ = xr[i % 2]
            for fc in range(DC):
                v.tensor_tensor(out=xb_[:, fc * TT:(fc + 1) * TT],
                                in0=xb_[:, fc * TT:(fc + 1) * TT],
                                in1=rB[i % 2][:], op=ALU.mult)
            v.drain()
            for fc in range(DC):
                dst = xn_own[:, fc * T + i * TT: fc * T + (i + 1) * TT] if own \
                    else xb_[:, fc * TT:(fc + 1) * TT]
                v.tensor_tensor(out=dst, in0=xb_[:, fc * TT:(fc + 1) * TT],
                                in1=mB[i % 2][:], op=ALU.add)
            v.drain()
            for fc in range(DC):
                src_ = xn_own[:, fc * T + i * TT: fc * T + (i + 1) * TT] if own \
                    else xb_[:, fc * TT:(fc + 1) * TT]
                ins = v.tensor_reduce(out=mxg[:, i * DC + fc: i * DC + fc + 1], in_=src_,
                                      axis=AX.X, op=ALU.max, apply_absolute_value=True)
            ins.then_inc(sVE.h, 1)
            ap_done[i] = sVE.inc()

        for i in range(NTILE):
            block.sync(lambda sync, i=i: p1_load(sync, i))
            block.scalar(lambda s, i=i: p1_sq(s, i))
            block.tensor(lambda t, i=i: p1_stats_mm(t, i))
            block.vector(lambda v, i=i: p1_stats_ev(v, i))
            block.gpsimd(lambda g, i=i: p1_rp_in(g, i))
            block.vector(lambda v, i=i: p1_strip1(v, i))
            block.scalar(lambda s, i=i: p1_sqrt(s, i))
            block.vector(lambda v, i=i: p1_strip2(v, i))
            block.gpsimd(lambda g, i=i: p1_rp_out(g, i))
            block.tensor(lambda t, i=i: p1_bcast(t, i))
            block.vector(lambda v, i=i: p1_bc_cp(v, i))
            block.vector(lambda v, i=i: p1_apply(v, i))

        def p1_fin_red(v):
            v.drain()
            v.tensor_reduce(out=strip[:, 0:1], in_=mxg[:, 0:NTILE * DC], axis=AX.X,
                            op=ALU.max).then_inc(sVE.h, 1)
            sVE.inc()
        block.vector(p1_fin_red)

        def p1_fin_tr(t):
            t.wait_ge(sVE.h, sVE.v)
            t.transpose(out=psB[0:1, 0:128], in_=strip[:, 0:1], identity=ident_f[:]) \
                .then_inc(sPE.h, 1)
            sPE.inc()
        block.tensor(p1_fin_tr)

        def p1_fin_scl(v):
            v.wait_ge(sPE.h, sPE.v)
            v.tensor_reduce(out=scl[0:1, 0:1], in_=psB[0:1, 0:128], axis=AX.X, op=ALU.max)
            v.drain()
            v.tensor_scalar(out=scl[0:1, 1:2], in0=scl[0:1, 0:1], scalar1=MINR,
                            scalar2=1.0 / QP, op0=ALU.max, op1=ALU.mult)
            v.drain()
            v.reciprocal(out=scl[0:1, 2:3], in_=scl[0:1, 1:2]).then_inc(sVE.h, 1)
            sVE.inc()
        block.vector(p1_fin_scl)
        state["P1_SCL"] = sVE.v

        d_sx = dbg_tensor("sx", [1, 8])
        d_xn = dbg_tensor("xn_own", [128, DC * T])
        d_stats = dbg_tensor("stats", [128, 256])
        if dbg_out:
            def p1_dbg(sync):
                sync.wait_ge(sVE.h, state["P1_SCL"])
                if d_sx is not None:
                    sync.dma_start(out=d_sx[:], in_=scl[0:1, 0:8]).then_inc(sDO.h, 16)
                    sDO.inc(16)
                if d_xn is not None:
                    sync.dma_start(out=d_xn[:], in_=xn_own[:]).then_inc(sDO.h, 16)
                    sDO.inc(16)
                if d_stats is not None:
                    sync.dma_start(out=d_stats[:], in_=stats_rp[:]).then_inc(sDO.h, 16)
                    sDO.inc(16)
            block.sync(p1_dbg)

        if phases < 2:
            def p1_out(sync):
                sync.wait_ge(sVE.h, state["P1_SCL"])
                sync.dma_start(out=y[0:128, 0:D], in_=xn_own[:, 0:D]).then_inc(sDO.h, 16)
                sDO.inc(16)
                sync.wait_ge(sDO.h, sDO.v)
            block.sync(p1_out)
            return nc, dbg_out

        ar.freebuf("xr0", "xr1", "sqf", "rB0", "rB1", "mB0", "mB1")
        xn_bf = ar.alloc("xn_bf", [128, DC * T], BF16)

        def p2_cast(v):
            for fc in range(DC):
                ins = v.tensor_copy(out=xn_bf[:, fc * T:(fc + 1) * T],
                                    in_=xn_own[:, fc * T:(fc + 1) * T])
            ins.then_inc(sVE.h, 1)
            state["xnbf"] = sVE.inc()
        block.vector(p2_cast)

        ar.freebuf("xn_own")
        Wqkv_sb = ar.alloc("Wqkv_sb", [128, 3 * DC * D], BF16)
        qT_bf = ar.alloc("qT_bf", [128, DC * T], BF16)
        kT_bf = ar.alloc("kT_bf", [128, DC * T], BF16)
        vch = [ar.alloc("vch0", [128, T], F32), ar.alloc("vch1", [128, T], F32)]
        sqsc = ar.alloc("sqsc", [128, T], F32)

        def p2_wload(g):
            g.wait_ge(sVE.h, state["xnbf"])
            for wi, W in enumerate((WqT, WkT, WvT)):
                g.dma_start(out=Wqkv_sb[:, wi * DC * D:(wi + 1) * DC * D], in_=W[:]) \
                    .then_inc(sWT.h, 16)
                sWT.inc(16)
        block.gpsimd(p2_wload)
        state["w_qkv"] = sWT.v

        qkv_mm = {}
        qkv_ev = {}
        slot_last_ev = [0, 0]
        vspill = [None] * DC

        def p2_qkv_mm(t, w, oc):
            idx = w * 8 + oc
            ps = psA if idx % 2 == 0 else psB
            if idx == 0:
                t.wait_ge(sWT.h, state["w_qkv"])
                t.wait_ge(sVE.h, state["xnbf"])
            if slot_last_ev[idx % 2]:
                t.wait_ge(sSE.h, slot_last_ev[idx % 2])
            base = w * DC * D
            for ic in range(DC):
                st_ = ic == 0
                sp_ = ic == DC - 1
                for nn_ in range(4):
                    ins = t.matmul(
                        out=ps[:, nn_ * 512:(nn_ + 1) * 512],
                        lhsT=Wqkv_sb[:, base + ic * D + oc * 128: base + ic * D + oc * 128 + 128],
                        rhs=xn_bf[:, ic * T + nn_ * 512: ic * T + (nn_ + 1) * 512],
                        start=st_, stop=sp_)
            ins.then_inc(sPE.h, 1)
            qkv_mm[(w, oc)] = sPE.inc()

        def p2_qkv_ev(s, w, oc):
            idx = w * 8 + oc
            ps = psA if idx % 2 == 0 else psB
            s.wait_ge(sPE.h, qkv_mm[(w, oc)])
            bias = bqkv_sb[:, oc * 3 + w: oc * 3 + w + 1]
            if w < 2:
                dst = (qT_bf if w == 0 else kT_bf)[:, oc * T:(oc + 1) * T]
                s.activation(out=dst, in_=ps[:, 0:T], func=AF.Identity, bias=bias)
                ins = s.activation(out=sqsc[:], in_=ps[:, 0:T], func=AF.Square, bias=bias,
                                   accum_out=strip[:, 16 + w * 8 + oc:17 + w * 8 + oc])
            else:
                if oc >= 2:
                    s.wait_ge(sDO.h, vspill[oc - 2])
                ins = s.activation(out=vch[oc % 2][:], in_=ps[:, 0:T], func=AF.Identity, bias=bias)
            ins.then_inc(sSE.h, 1)
            qkv_ev[(w, oc)] = slot_last_ev[idx % 2] = sSE.inc()

        def p2_vspill(sync, oc):
            sync.wait_ge(sSE.h, qkv_ev[(2, oc)])
            sync.dma_start(out=v_hbm[oc], in_=vch[oc % 2][:]).then_inc(sDO.h, 16)
            vspill[oc] = sDO.inc(16)

        for w in range(3):
            for oc in range(DC):
                block.tensor(lambda t, w=w, oc=oc: p2_qkv_mm(t, w, oc))
                block.scalar(lambda s, w=w, oc=oc: p2_qkv_ev(s, w, oc))
                if w == 2:
                    block.sync(lambda sync, oc=oc: p2_vspill(sync, oc))

        ar.freebuf("xn_bf", "Wqkv_sb")
        phiq_f = ar.alloc("phiq_f", [64, T], F32)
        phik_f = ar.alloc("phik_f", [64, T], F32)
        prod_f = ar.alloc("prod_f", [64, T], F32)
        s_all = ar.alloc("s_all", [16, T], F32)
        rs_all = ar.alloc("rs_all", [16, T], F32)
        sfrag = ar.alloc("sfrag", [1, T], F32)
        srp_ = ar.alloc("srp_", [128, 256], F32)
        s2 = ar.alloc("s2", [2, T], F32)
        rs2 = ar.alloc("rs2", [2, T], F32)

        def p2_norm_mm(t):
            t.wait_ge(sSE.h, sSE.v)
            t.matmul(out=psA[0:1, 0:16], lhsT=ones_col[:, 0:1], rhs=strip[:, 16:32],
                     start=True, stop=True).then_inc(sPE.h, 1)
            state["nmm"] = sPE.inc()
        block.tensor(p2_norm_mm)

        LN8 = math.log(8.0)

        def p2_norm_str(v):
            v.wait_ge(sPE.h, state["nmm"])
            v.tensor_copy(out=scl[0:1, 32:48], in_=psA[0:1, 0:16])
            v.drain()
            v.tensor_reduce(out=scl[0:1, 16:17], in_=scl[0:1, 32:40], axis=AX.X, op=ALU.add)
            ins = v.tensor_reduce(out=scl[0:1, 17:18], in_=scl[0:1, 40:48], axis=AX.X, op=ALU.add)
            ins.then_inc(sVE.h, 1)
            state["nsum"] = sVE.inc()
        block.vector(p2_norm_str)

        def p2_norm_sqrt(s):
            s.wait_ge(sVE.h, state["nsum"])
            s.sqrt(out=scl[0:1, 20:22], in_=scl[0:1, 16:18]).then_inc(sSE.h, 1)
            state["nsq"] = sSE.inc()
        block.scalar(p2_norm_sqrt)

        def p2_norm_fin(v):
            v.wait_ge(sSE.h, state["nsq"])
            v.reciprocal(out=scl[0:1, 18:20], in_=scl[0:1, 20:22])
            v.drain()
            v.tensor_tensor(out=scl[0:1, 26:28], in0=scl[0:1, 16:18], in1=scl[0:1, 18:20],
                            op=ALU.mult)
            v.drain()
            v.tensor_tensor(out=scl[0:1, 26:28], in0=scl[0:1, 26:28], in1=scl[0:1, 18:20],
                            op=ALU.mult)
            v.drain()
            v.tensor_scalar(out=scl[0:1, 22:24], in0=scl[0:1, 26:28], scalar1=-0.5,
                            scalar2=-LN8, op0=ALU.mult, op1=ALU.add).then_inc(sVE.h, 1)
            state["nfin"] = sVE.inc()
        block.vector(p2_norm_fin)

        def p2_sclB(t):
            t.wait_ge(sVE.h, state["nfin"])
            for j, col in enumerate((18, 19, 22, 23)):
                t.matmul(out=psA[:, 16 + j:17 + j], lhsT=ones_row[0:1, :],
                         rhs=scl[0:1, col:col + 1], start=True, stop=True).then_inc(sPE.h, 1)
                sPE.inc()
            state["sclBmm"] = sPE.v
        block.tensor(p2_sclB)

        def p2_sclB_cp(v):
            v.wait_ge(sPE.h, state["sclBmm"])
            for j in range(4):
                ins = v.tensor_copy(out=sclB[:, j:j + 1], in_=psA[:, 16 + j:17 + j])
            ins.then_inc(sVE.h, 1)
            state["sclB4"] = sVE.inc()
        block.vector(p2_sclB_cp)

        phi_mm = [None] * 16
        phi_ev = [None] * 16
        phi_pr = [None] * 16
        phi_sm = [None] * 16
        ssum_cp = [None] * 16
        sdma = [None] * 16

        def p2_phi_mm(t, h):
            pr = (h % 2) * 64
            oc = h // 2
            if h == 0:
                t.wait_ge(sVE.h, state["sclB4"])
            else:
                t.wait_ge(sSE.h, phi_ev[h - 1])
                t.wait_ge(sVE.h, ssum_cp[h - 1])
            for nn_ in range(4):
                t.matmul(out=psA[0:64, nn_ * 512:(nn_ + 1) * 512],
                         lhsT=phiW_sb[pr:pr + 64, h * 64:(h + 1) * 64],
                         rhs=qT_bf[pr:pr + 64, oc * T + nn_ * 512: oc * T + (nn_ + 1) * 512],
                         start=True, stop=True)
            for nn_ in range(4):
                ins = t.matmul(out=psB[0:64, nn_ * 512:(nn_ + 1) * 512],
                               lhsT=phiW_sb[pr:pr + 64, h * 64:(h + 1) * 64],
                               rhs=kT_bf[pr:pr + 64, oc * T + nn_ * 512: oc * T + (nn_ + 1) * 512],
                               start=True, stop=True)
            ins.then_inc(sPE.h, 1)
            phi_mm[h] = sPE.inc()

        def p2_phi_ev(s, h):
            s.wait_ge(sPE.h, phi_mm[h])
            s.activation(out=phiq_f[:], in_=psA[0:64, 0:T], func=AF.Exp,
                         scale=sclB[0:64, 0:1], bias=sclB[0:64, 2:3])
            ins = s.activation(out=phik_f[:], in_=psB[0:64, 0:T], func=AF.Exp,
                               scale=sclB[0:64, 1:2], bias=sclB[0:64, 3:4])
            ins.then_inc(sSE.h, 1)
            phi_ev[h] = sSE.inc()

        def p2_phi_prod(v, h):
            v.wait_ge(sSE.h, phi_ev[h])
            v.tensor_tensor(out=prod_f[:], in0=phiq_f[:], in1=phik_f[:],
                            op=ALU.mult).then_inc(sVE.h, 1)
            phi_pr[h] = sVE.inc()

        def p2_phi_ssum(t, h):
            t.wait_ge(sVE.h, phi_pr[h])
            for nn_ in range(4):
                ins = t.matmul(out=psA[0:1, nn_ * 512:(nn_ + 1) * 512],
                               lhsT=ones_col[0:64, 0:1],
                               rhs=prod_f[:, nn_ * 512:(nn_ + 1) * 512],
                               start=True, stop=True)
            ins.then_inc(sPE.h, 1)
            phi_sm[h] = sPE.inc()

        def p2_phi_scp(v, h):
            v.wait_ge(sPE.h, phi_sm[h])
            if h >= 1:
                v.wait_ge(sGC.h, sdma[h - 1])
            v.tensor_copy(out=sfrag[0:1, :], in_=psA[0:1, 0:T]).then_inc(sVE.h, 1)
            ssum_cp[h] = sVE.inc()

        def p2_phi_sdma(g, h):
            g.wait_ge(sVE.h, ssum_cp[h])
            g.dma_start(out=s_all[h:h + 1, :], in_=sfrag[0:1, :]).then_inc(sGC.h, 16)
            sdma[h] = sGC.inc(16)

        for h in range(16):
            block.tensor(lambda t, h=h: p2_phi_mm(t, h))
            block.scalar(lambda s, h=h: p2_phi_ev(s, h))
            block.vector(lambda v, h=h: p2_phi_prod(v, h))
            block.tensor(lambda t, h=h: p2_phi_ssum(t, h))
            block.vector(lambda v, h=h: p2_phi_scp(v, h))
            block.gpsimd(lambda g, h=h: p2_phi_sdma(g, h))

        def p2_srp_in(g):
            g.wait_ge(sGC.h, sdma[15])
            g.dma_start(out=srp_[:], in_=s_all[0:16, :]).then_inc(sGC.h, 16)
            state["srpin"] = sGC.inc(16)
        block.gpsimd(p2_srp_in)

        def p2_srecip(v):
            v.wait_ge(sGC.h, state["srpin"])
            v.reciprocal(out=srp_[:, 0:256], in_=srp_[:, 0:256]).then_inc(sVE.h, 1)
            state["srec"] = sVE.inc()
        block.vector(p2_srecip)

        def p2_srp_out(g):
            g.wait_ge(sVE.h, state["srec"])
            g.dma_start(out=rs_all[0:16, :], in_=srp_[:]).then_inc(sGC.h, 16)
            state["srpout"] = sGC.inc(16)
        block.gpsimd(p2_srp_out)

        ar.freebuf("phiq_f", "phik_f", "prod_f")
        uf = ar.alloc("uf", [128, T], F32)
        x0r = [ar.alloc("x0r0", [128, T], F32), ar.alloc("x0r1", [128, T], F32)]
        sv_mm = [None] * DC
        sv_ld = [None] * DC
        sv_tt = [None] * DC
        sv_sp = [None] * DC

        sv_stg = [None] * DC

        def p2_sv_stg(g, fc):
            if fc == 0:
                g.wait_ge(sGC.h, state["srpout"])
            if fc >= 1:
                g.wait_ge(sPE.h, sv_mm[fc - 1])
            for r in range(2):
                h = 2 * fc + r
                g.dma_start(out=s2[r:r + 1, :], in_=s_all[h:h + 1, :]).then_inc(sGC.h, 16)
                g.dma_start(out=rs2[r:r + 1, :], in_=rs_all[h:h + 1, :]).then_inc(sGC.h, 16)
            sv_stg[fc] = sGC.inc(64)

        def p2_sv_mm(t, fc):
            t.wait_ge(sGC.h, sv_stg[fc])
            if fc == 0:
                t.wait_ge(sVE.h, ssum_cp[15])
            else:
                t.wait_ge(sVE.h, sv_tt[fc - 1])
            for nn_ in range(4):
                t.matmul(out=psA[:, nn_ * 512:(nn_ + 1) * 512],
                         lhsT=sel_sb[0:2, :],
                         rhs=s2[0:2, nn_ * 512:(nn_ + 1) * 512],
                         start=True, stop=True)
            for nn_ in range(4):
                ins = t.matmul(out=psB[:, nn_ * 512:(nn_ + 1) * 512],
                               lhsT=sel_sb[0:2, :],
                               rhs=rs2[0:2, nn_ * 512:(nn_ + 1) * 512],
                               start=True, stop=True)
            ins.then_inc(sPE.h, 1)
            sv_mm[fc] = sPE.inc()

        def p2_sv_ld(sync, fc):
            if fc < 2:
                sync.wait_ge(sDO.h, vspill[DC - 1])
                sync.wait_ge(sPE.h, phi_sm[15])
                sync.wait_ge(sVE.h, ssum_cp[15])
            if fc >= 2:
                sync.wait_ge(sVE.h, sv_tt[fc - 2])
                sync.wait_ge(sDO.h, sv_sp[fc - 2])
            sync.dma_start(out=vch[fc % 2][:], in_=v_hbm[fc]).then_inc(sIN.h, 16)
            sync.dma_start(out=x0r[fc % 2][:, 0:TT], in_=xT_t[0][:, fc * TT:(fc + 1) * TT]) \
                .then_inc(sIN.h, 16)
            sync.dma_start(out=x0r[fc % 2][:, TT:T], in_=xT_t[1][:, fc * TT:(fc + 1) * TT]) \
                .then_inc(sIN.h, 16)
            sv_ld[fc] = sIN.inc(48)

        def p2_sv_tt(v, fc):
            v.wait_ge(sPE.h, sv_mm[fc])
            v.wait_ge(sIN.h, sv_ld[fc])
            if fc >= 2:
                v.wait_ge(sDO.h, sv_sp[fc - 2])
            v.tensor_tensor(out=uf[:], in0=vch[fc % 2][:], in1=psA[:, 0:T], op=ALU.mult)
            v.drain()
            v.tensor_tensor(out=uf[:], in0=uf[:], in1=psB[:, 0:T], op=ALU.mult)
            v.drain()
            v.tensor_tensor(out=x0r[fc % 2][:], in0=uf[:], in1=x0r[fc % 2][:],
                            op=ALU.add).then_inc(sVE.h, 1)
            sv_tt[fc] = sVE.inc()

        def p2_sv_sp(sync, fc):
            sync.wait_ge(sVE.h, sv_tt[fc])
            sync.dma_start(out=x1p_hbm[fc], in_=x0r[fc % 2][:]).then_inc(sDO.h, 16)
            sv_sp[fc] = sDO.inc(16)

        for fc in range(DC):
            block.sync(lambda sync, fc=fc: p2_sv_ld(sync, fc))
            block.gpsimd(lambda g, fc=fc: p2_sv_stg(g, fc))
            block.tensor(lambda t, fc=fc: p2_sv_mm(t, fc))
            block.vector(lambda v, fc=fc: p2_sv_tt(v, fc))
            block.sync(lambda sync, fc=fc: p2_sv_sp(sync, fc))

        state["P2_END_V"] = sVE.v
        state["P2_END_S"] = sSE.v
        state["P2_END_DO"] = sDO.v

        d_sall = dbg_tensor("s_all", [16, T])
        if d_sall is not None:
            def p2_dbg(sync):
                sync.wait_ge(sGC.h, state["srpin"])
                sync.dma_start(out=d_sall[:], in_=s_all[0:16, :]).then_inc(sDO.h, 16)
                sDO.inc(16)
            block.sync(p2_dbg)

        ar.freebuf("qT_bf", "kT_bf", "vch0", "vch1", "sqsc",
                   "s_all", "rs_all", "sfrag", "srp_", "s2", "rs2",
                   "uf", "x0r0", "x0r1")
        wq1_bf = ar.alloc("wq1_bf", [128, DC * DH], BF16)
        w1sl = [ar.alloc("w1sl0", [128, DH], F32), ar.alloc("w1sl1", [128, DH], F32)]
        w1qi = ar.alloc("w1qi", [128, DH], I32)
        fmg = ar.alloc("fmg", [128, 1024], F32)
        w2grid_col = 48

        def p3_sclb4(t):
            t.wait_ge(sVE.h, state["P2_END_V"])
            t.matmul(out=psA[:, 16:17], lhsT=ones_row[0:1, :], rhs=scl[0:1, 2:3],
                     start=True, stop=True).then_inc(sPE.h, 1)
            state["p3b4"] = sPE.inc()
        block.tensor(p3_sclb4)

        def p3_sclb4_cp(v):
            v.wait_ge(sPE.h, state["p3b4"])
            v.tensor_copy(out=sclB[:, 4:5], in_=psA[:, 16:17]).then_inc(sVE.h, 1)
            state["p3b4c"] = sVE.inc()
        block.vector(p3_sclb4_cp)

        w1_ld = [None] * DC
        w1_sc = [None] * DC

        def p3_w1_ld(g, ic):
            if ic == 0:
                g.wait_ge(sVE.h, state["P2_END_V"])
                g.wait_ge(sSE.h, state["P2_END_S"])
            if ic >= 2:
                g.wait_ge(sVE.h, w1_sc[ic - 2])
            g.dma_start(out=w1sl[ic % 2][:], in_=w1T_t[:, ic * DH:(ic + 1) * DH]) \
                .then_inc(sWT.h, 16)
            w1_ld[ic] = sWT.inc(16)

        def p3_w1_sc(v, ic):
            v.wait_ge(sWT.h, w1_ld[ic])
            v.tensor_reduce(out=strip[:, w2grid_col + ic:w2grid_col + ic + 1],
                            in_=w1sl[ic % 2][:], axis=AX.X, op=ALU.max,
                            apply_absolute_value=True).then_inc(sVE.h, 1)
            w1_sc[ic] = sVE.inc()

        for ic in range(DC):
            block.gpsimd(lambda g, ic=ic: p3_w1_ld(g, ic))
            block.vector(lambda v, ic=ic: p3_w1_sc(v, ic))

        def p3_w1_red(v):
            v.drain()
            v.tensor_reduce(out=strip[:, 56:57], in_=strip[:, w2grid_col:w2grid_col + DC],
                            axis=AX.X, op=ALU.max).then_inc(sVE.h, 1)
            state["w1red"] = sVE.inc()
        block.vector(p3_w1_red)

        def p3_w1_tr(t):
            t.wait_ge(sVE.h, state["w1red"])
            t.transpose(out=psB[0:1, 0:128], in_=strip[:, 56:57], identity=ident_f[:]) \
                .then_inc(sPE.h, 1)
            state["w1tr"] = sPE.inc()
        block.tensor(p3_w1_tr)

        def p3_w1_fin(v):
            v.wait_ge(sPE.h, state["w1tr"])
            v.tensor_reduce(out=scl[0:1, 3:4], in_=psB[0:1, 0:128], axis=AX.X, op=ALU.max)
            v.drain()
            v.tensor_scalar(out=scl[0:1, 4:5], in0=scl[0:1, 3:4], scalar1=MINR,
                            scalar2=1.0 / QP, op0=ALU.max, op1=ALU.mult)
            v.drain()
            v.reciprocal(out=scl[0:1, 5:6], in_=scl[0:1, 4:5])
            v.tensor_tensor(out=scl[0:1, 6:7], in0=scl[0:1, 4:5], in1=scl[0:1, 1:2],
                            op=ALU.mult).then_inc(sVE.h, 1)
            state["w1fin"] = sVE.inc()
        block.vector(p3_w1_fin)

        def p3_w1_bc(t):
            t.wait_ge(sVE.h, state["w1fin"])
            t.matmul(out=psA[:, 17:18], lhsT=ones_row[0:1, :], rhs=scl[0:1, 5:6],
                     start=True, stop=True)
            t.matmul(out=psA[:, 18:19], lhsT=ones_row[0:1, :], rhs=scl[0:1, 6:7],
                     start=True, stop=True).then_inc(sPE.h, 1)
            state["w1bc"] = sPE.inc()
        block.tensor(p3_w1_bc)

        def p3_w1_bc_cp(v):
            v.wait_ge(sPE.h, state["w1bc"])
            v.tensor_copy(out=sclB[:, 6:7], in_=psA[:, 17:18])
            v.tensor_copy(out=sclB[:, 5:6], in_=psA[:, 18:19])
            v.drain()
            v.tensor_scalar(out=b1c[:], in0=b1_sb[:], scalar1=sclB[:, 5:6], scalar2=None,
                            op0=ALU.mult).then_inc(sVE.h, 1)
            state["w1bcc"] = sVE.inc()
        block.vector(p3_w1_bc_cp)

        w1_ld2 = [None] * DC
        w1_q = [None] * DC

        def p3_w1_ld2(g, ic):
            if ic == 0:
                g.wait_ge(sVE.h, state["w1fin"])
            if ic >= 2:
                g.wait_ge(sVE.h, w1_q[ic - 2])
            g.dma_start(out=w1sl[ic % 2][:], in_=w1T_t[:, ic * DH:(ic + 1) * DH]) \
                .then_inc(sWT.h, 16)
            w1_ld2[ic] = sWT.inc(16)

        def p3_w1_q(v, ic):
            v.wait_ge(sWT.h, w1_ld2[ic])
            v.wait_ge(sVE.h, state["w1bcc"])
            v.tensor_scalar(out=w1qi[:], in0=w1sl[ic % 2][:], scalar1=sclB[:, 6:7],
                            scalar2=None, op0=ALU.mult)
            v.drain()
            v.tensor_copy(out=wq1_bf[:, ic * DH:(ic + 1) * DH], in_=w1qi[:]) \
                .then_inc(sVE.h, 1)
            w1_q[ic] = sVE.inc()

        for ic in range(DC):
            block.gpsimd(lambda g, ic=ic: p3_w1_ld2(g, ic))
            block.vector(lambda v, ic=ic: p3_w1_q(v, ic))
        state["wq1"] = w1_q[DC - 1]

        ar.freebuf("w1sl0", "w1sl1", "w1qi")
        xsub = [ar.alloc("xsub0", [128, DC * 512], F32), ar.alloc("xsub1", [128, DC * 512], F32)]
        xqi = ar.alloc("xqi", [128, DC * 512], I32)
        xqb = [ar.alloc("xqb0", [128, DC * 512], BF16), ar.alloc("xqb1", [128, DC * 512], BF16)]
        rB2 = ar.alloc("rB2", [128, 512], F32)
        mB2 = ar.alloc("mB2", [128, 512], F32)
        hst = [ar.alloc("hst0", [128, 512], F32), ar.alloc("hst1", [128, 512], F32)]

        NSUB = 32
        x_ld3 = [None] * NSUB
        st_bc3 = {}
        bc_cp3 = [None] * NSUB
        stage2_d = [None] * NSUB
        stage3_d = [None] * NSUB
        mm_grp = {}
        red_d = {}
        ev_d = {}
        hsp = {}
        slot_red = [0, 0, 0, 0]
        slot_ev = [0, 0, 0, 0]
        srow_rd = [0] * NTILE
        srow_wr = [0] * NTILE

        def p3_xld(sync, sub):
            if sub == 0:
                sync.wait_ge(sVE.h, state["wq1"])
            if sub >= 2:
                sync.wait_ge(sVE.h, stage2_d[sub - 2])
            tile = sub // 2
            half = sub % 2
            src = xT_t[tile].rearrange("p (a c) -> p a c", c=TT)[:, :, half * 512:(half + 1) * 512]
            dst = xsub[sub % 2].rearrange("p (a c) -> p a c", c=512)
            sync.dma_start(out=dst, in_=src).then_inc(sIN.h, 16)
            x_ld3[sub] = sIN.inc(16)

        def p3_stats_rp(g, sub):
            tile = sub // 2
            if sub % 2 != 0:
                return
            if tile >= 1:
                g.wait_ge(sPE.h, srow_rd[tile - 1])
            g.dma_start(out=srow[0:1, 0:TT], in_=stats_rp[:, tile * 8:(tile + 1) * 8]) \
                .then_inc(sGC.h, 16)
            g.dma_start(out=srow[0:1, TT:2 * TT],
                        in_=stats_rp[:, 128 + tile * 8:128 + (tile + 1) * 8]) \
                .then_inc(sGC.h, 16)
            srow_wr[tile] = sGC.inc(32)

        def p3_stats_bc(t, sub):
            tile = sub // 2
            half = sub % 2
            t.wait_ge(sGC.h, srow_wr[tile])
            if sub >= 1:
                t.wait_ge(sVE.h, bc_cp3[sub - 1])
            t.matmul(out=psB[:, 0:512], lhsT=ones_row[0:1, :],
                     rhs=srow[0:1, half * 512:(half + 1) * 512], start=True, stop=True)
            ins = t.matmul(out=psB[:, 512:1024], lhsT=ones_row[0:1, :],
                           rhs=srow[0:1, TT + half * 512:TT + (half + 1) * 512],
                           start=True, stop=True)
            ins.then_inc(sPE.h, 1)
            st_bc3[sub] = srow_rd[tile] = sPE.inc()

        def p3_stats_cp(v, sub):
            v.wait_ge(sPE.h, st_bc3[sub])
            if sub == 0:
                v.wait_ge(sVE.h, state["p3b4c"])
            v.tensor_scalar(out=rB2[:], in0=psB[:, 0:512], scalar1=sclB[:, 4:5],
                            scalar2=None, op0=ALU.mult)
            ins = v.tensor_scalar(out=mB2[:], in0=psB[:, 512:1024], scalar1=sclB[:, 4:5],
                                  scalar2=None, op0=ALU.mult)
            ins.then_inc(sVE.h, 1)
            bc_cp3[sub] = sVE.inc()

        def p3_apply(v, sub):
            v.wait_ge(sIN.h, x_ld3[sub])
            v.wait_ge(sVE.h, bc_cp3[sub])
            xb_ = xsub[sub % 2]
            for fc in range(DC):
                v.tensor_tensor(out=xb_[:, fc * 512:(fc + 1) * 512],
                                in0=xb_[:, fc * 512:(fc + 1) * 512], in1=rB2[:], op=ALU.mult)
            v.drain()
            for fc in range(DC):
                ins = v.tensor_tensor(out=xqi[:, fc * 512:(fc + 1) * 512],
                                      in0=xb_[:, fc * 512:(fc + 1) * 512], in1=mB2[:],
                                      op=ALU.add)
            ins.then_inc(sVE.h, 1)
            stage2_d[sub] = sVE.inc()
            v.drain()
            for fc in range(DC):
                ins = v.tensor_copy(out=xqb[sub % 2][:, fc * 512:(fc + 1) * 512],
                                    in_=xqi[:, fc * 512:(fc + 1) * 512])
            ins.then_inc(sVE.h, 1)
            stage3_d[sub] = sVE.inc()

        def p3_ffn_mm(t, sub, oc):
            slot = oc % 4
            if sub == 0 and oc == 0:
                t.wait_ge(sVE.h, state["wq1"])
            t.wait_ge(sVE.h, max(stage3_d[sub], slot_red[slot]))
            if slot_ev[slot]:
                t.wait_ge(sSE.h, slot_ev[slot])
            for ic in range(DC):
                ins = t.matmul(out=psA[:, slot * 512:(slot + 1) * 512],
                               lhsT=wq1_bf[:, ic * DH + oc * 128: ic * DH + oc * 128 + 128],
                               rhs=xqb[sub % 2][:, ic * 512:(ic + 1) * 512],
                               start=(ic == 0), stop=(ic == DC - 1))
            ins.then_inc(sPE.h, 1)
            mm_grp[(sub, oc)] = sPE.inc()

        def p3_ffn_red(v, sub, oc):
            slot = oc % 4
            v.wait_ge(sPE.h, mm_grp[(sub, oc)])
            v.tensor_reduce(out=fmg[:, oc * NSUB + sub: oc * NSUB + sub + 1],
                            in_=psA[:, slot * 512:(slot + 1) * 512],
                            axis=AX.X, op=ALU.max).then_inc(sVE.h, 1)
            red_d[(sub, oc)] = slot_red[oc % 4] = sVE.inc()

        hsp_list = []

        def p3_ffn_ev(s, sub, oc):
            slot = oc % 4
            s.wait_ge(sPE.h, mm_grp[(sub, oc)])
            k = len(hsp_list)
            if k >= 2:
                s.wait_ge(sDO.h, hsp_list[k - 2])
            ins = s.activation(out=hst[k % 2][:], in_=psA[:, slot * 512:(slot + 1) * 512],
                               func=AF.Gelu, scale=sclB[:, 5:6], bias=b1c[:, oc:oc + 1])
            ins.then_inc(sSE.h, 1)
            ev_d[(sub, oc)] = slot_ev[slot] = sSE.inc()

        def p3_ffn_hsp(sync, sub, oc):
            k = len(hsp_list)
            sync.wait_ge(sSE.h, ev_d[(sub, oc)])
            sync.dma_start(out=h_own[oc][:, sub * 512:(sub + 1) * 512], in_=hst[k % 2][:]) \
                .then_inc(sDO.h, 16)
            hsp_list.append(sDO.inc(16))

        for sub in range(NSUB):
            block.sync(lambda sync, sub=sub: p3_xld(sync, sub))
            block.gpsimd(lambda g, sub=sub: p3_stats_rp(g, sub))
            block.tensor(lambda t, sub=sub: p3_stats_bc(t, sub))
            block.vector(lambda v, sub=sub: p3_stats_cp(v, sub))
            block.vector(lambda v, sub=sub: p3_apply(v, sub))
            own = sub < 4
            for oc in range(HC):
                block.tensor(lambda t, sub=sub, oc=oc: p3_ffn_mm(t, sub, oc))
                block.vector(lambda v, sub=sub, oc=oc: p3_ffn_red(v, sub, oc))
                if own:
                    block.scalar(lambda s, sub=sub, oc=oc: p3_ffn_ev(s, sub, oc))
                    block.sync(lambda sync, sub=sub, oc=oc: p3_ffn_hsp(sync, sub, oc))

        def p3_fm_b1(v):
            v.drain()
            for oc in range(HC):
                ins = v.tensor_scalar(out=fmg[:, oc * NSUB:(oc + 1) * NSUB],
                                      in0=fmg[:, oc * NSUB:(oc + 1) * NSUB],
                                      scalar1=b1_sb[:, oc:oc + 1], scalar2=None, op0=ALU.add)
            v.drain()
            v.tensor_reduce(out=strip[:, 57:58], in_=fmg[:], axis=AX.X, op=ALU.max) \
                .then_inc(sVE.h, 1)
            state["fmred"] = sVE.inc()
        block.vector(p3_fm_b1)

        def p3_fm_tr(t):
            t.wait_ge(sVE.h, state["fmred"])
            t.transpose(out=psB[0:1, 0:128], in_=strip[:, 57:58], identity=ident_f[:]) \
                .then_inc(sPE.h, 1)
            state["fmtr"] = sPE.inc()
        block.tensor(p3_fm_tr)

        def p3_fm_fin1(v):
            v.wait_ge(sPE.h, state["fmtr"])
            v.tensor_reduce(out=scl[0:1, 7:8], in_=psB[0:1, 0:128], axis=AX.X, op=ALU.max)
            v.drain()
            v.tensor_tensor(out=scl[0:1, 28:29], in0=scl[0:1, 7:8], in1=scl[0:1, 6:7],
                            op=ALU.mult).then_inc(sVE.h, 1)
            state["fmf1"] = sVE.inc()
        block.vector(p3_fm_fin1)

        def p3_fm_gelu(s):
            s.wait_ge(sVE.h, state["fmf1"])
            s.activation(out=scl[0:1, 8:9], in_=scl[0:1, 28:29], func=AF.Gelu) \
                .then_inc(sSE.h, 1)
            state["fmg_"] = sSE.inc()
        block.scalar(p3_fm_gelu)

        def p3_fm_fin2(v):
            v.wait_ge(sSE.h, state["fmg_"])
            v.tensor_scalar(out=scl[0:1, 9:10], in0=scl[0:1, 8:9], scalar1=MINR,
                            scalar2=1.0 / QP, op0=ALU.max, op1=ALU.mult)
            v.drain()
            v.reciprocal(out=scl[0:1, 10:11], in_=scl[0:1, 9:10]).then_inc(sVE.h, 1)
            state["P3_SCL"] = sVE.inc()
        block.vector(p3_fm_fin2)

        d_scl = dbg_tensor("scl", [1, 64])
        d_hown = dbg_tensor("h_own", [HC, 128, T])
        d_x1p = dbg_tensor("x1p", [DC, 128, T])
        if d_scl is not None or d_hown is not None or d_x1p is not None:
            def p3_dbg(sync):
                sync.wait_ge(sVE.h, state["P3_SCL"])
                sync.wait_ge(sDO.h, sDO.v)
                if d_scl is not None:
                    sync.dma_start(out=d_scl[:], in_=scl[0:1, 0:64]).then_inc(sDO.h, 16)
                    sDO.inc(16)
                if d_hown is not None:
                    for oc in range(HC):
                        sync.dma_start(out=d_hown[oc], in_=h_own[oc]).then_inc(sDO.h, 16)
                        sDO.inc(16)
                if d_x1p is not None:
                    for fc in range(DC):
                        sync.dma_start(out=d_x1p[fc], in_=x1p_hbm[fc]).then_inc(sDO.h, 16)
                        sDO.inc(16)
            block.sync(p3_dbg)

        if phases < 5:
            def p3_out(sync):
                sync.wait_ge(sVE.h, state["P3_SCL"])
                sync.dma_start(out=y[0:128, 0:512], in_=rB2[:, 0:512]).then_inc(sDO.h, 16)
                sDO.inc(16)
                sync.wait_ge(sDO.h, sDO.v)
            block.sync(p3_out)
            return nc, dbg_out

        ar.freebuf("wq1_bf", "xsub0", "xsub1", "xqi", "xqb0", "xqb1", "fmg")
        wq2_bf = ar.alloc("wq2_bf", [128, HC * D], BF16)
        w2sl = [ar.alloc("w2sl0", [128, 4 * D], F32), ar.alloc("w2sl1", [128, 4 * D], F32)]
        w2qi = ar.alloc("w2qi", [128, 4 * D], I32)

        w2_ld = [None] * 8
        w2_sc = [None] * 8
        w2_ld2 = [None] * 8
        w2_q = [None] * 8

        def p5_w2_ld(g, i_):
            if i_ == 0:
                g.wait_ge(sVE.h, state["P3_SCL"])
                g.wait_ge(sSE.h, sSE.v)
                g.wait_ge(sPE.h, sPE.v)
            if i_ >= 2:
                g.wait_ge(sVE.h, w2_sc[i_ - 2])
            g.dma_start(out=w2sl[i_ % 2][:], in_=w2T_t[:, i_ * 4 * D:(i_ + 1) * 4 * D]) \
                .then_inc(sWT.h, 16)
            w2_ld[i_] = sWT.inc(16)

        def p5_w2_sc(v, i_):
            v.wait_ge(sWT.h, w2_ld[i_])
            v.tensor_reduce(out=strip[:, 48 + i_:49 + i_], in_=w2sl[i_ % 2][:],
                            axis=AX.X, op=ALU.max,
                            apply_absolute_value=True).then_inc(sVE.h, 1)
            w2_sc[i_] = sVE.inc()

        for i_ in range(8):
            block.gpsimd(lambda g, i_=i_: p5_w2_ld(g, i_))
            block.vector(lambda v, i_=i_: p5_w2_sc(v, i_))

        def p5_w2_red(v):
            v.drain()
            v.tensor_reduce(out=strip[:, 56:57], in_=strip[:, 48:56], axis=AX.X,
                            op=ALU.max).then_inc(sVE.h, 1)
            state["w2red"] = sVE.inc()
        block.vector(p5_w2_red)

        def p5_w2_tr(t):
            t.wait_ge(sVE.h, state["w2red"])
            t.transpose(out=psB[0:1, 0:128], in_=strip[:, 56:57], identity=ident_f[:]) \
                .then_inc(sPE.h, 1)
            state["w2tr"] = sPE.inc()
        block.tensor(p5_w2_tr)

        def p5_w2_fin(v):
            v.wait_ge(sPE.h, state["w2tr"])
            v.tensor_reduce(out=scl[0:1, 11:12], in_=psB[0:1, 0:128], axis=AX.X, op=ALU.max)
            v.drain()
            v.tensor_scalar(out=scl[0:1, 12:13], in0=scl[0:1, 11:12], scalar1=MINR,
                            scalar2=1.0 / QP, op0=ALU.max, op1=ALU.mult)
            v.drain()
            v.reciprocal(out=scl[0:1, 13:14], in_=scl[0:1, 12:13])
            v.tensor_tensor(out=scl[0:1, 14:15], in0=scl[0:1, 12:13], in1=scl[0:1, 9:10],
                            op=ALU.mult).then_inc(sVE.h, 1)
            state["w2fin"] = sVE.inc()
        block.vector(p5_w2_fin)

        def p5_w2_bc(t):
            t.wait_ge(sVE.h, state["w2fin"])
            for j, col in enumerate((13, 14, 10)):
                ins = t.matmul(out=psA[:, 16 + j:17 + j], lhsT=ones_row[0:1, :],
                               rhs=scl[0:1, col:col + 1], start=True, stop=True)
            ins.then_inc(sPE.h, 1)
            state["w2bc"] = sPE.inc()
        block.tensor(p5_w2_bc)

        def p5_w2_bc_cp(v):
            v.wait_ge(sPE.h, state["w2bc"])
            for j in range(3):
                v.tensor_copy(out=sclB[:, 7 + j:8 + j], in_=psA[:, 16 + j:17 + j])
            v.drain()
            v.tensor_scalar(out=b2c[:], in0=b2_sb[:], scalar1=sclB[:, 8:9], scalar2=None,
                            op0=ALU.mult).then_inc(sVE.h, 1)
            state["w2bcc"] = sVE.inc()
        block.vector(p5_w2_bc_cp)

        def p5_w2_ld2(g, i_):
            if i_ == 0:
                g.wait_ge(sVE.h, state["w2fin"])
            if i_ >= 2:
                g.wait_ge(sVE.h, w2_q[i_ - 2])
            g.dma_start(out=w2sl[i_ % 2][:], in_=w2T_t[:, i_ * 4 * D:(i_ + 1) * 4 * D]) \
                .then_inc(sWT.h, 16)
            w2_ld2[i_] = sWT.inc(16)

        def p5_w2_q(v, i_):
            v.wait_ge(sWT.h, w2_ld2[i_])
            v.wait_ge(sVE.h, state["w2bcc"])
            v.tensor_scalar(out=w2qi[:], in0=w2sl[i_ % 2][:], scalar1=sclB[:, 7:8],
                            scalar2=None, op0=ALU.mult)
            v.drain()
            v.tensor_copy(out=wq2_bf[:, i_ * 4 * D:(i_ + 1) * 4 * D], in_=w2qi[:]) \
                .then_inc(sVE.h, 1)
            w2_q[i_] = sVE.inc()

        for i_ in range(8):
            block.gpsimd(lambda g, i_=i_: p5_w2_ld2(g, i_))
            block.vector(lambda v, i_=i_: p5_w2_q(v, i_))
        state["wq2"] = w2_q[7]

        ar.freebuf("w2sl0", "w2sl1", "w2qi", "rB2", "mB2", "hst0", "hst1")
        hq_bf = ar.alloc("hq_bf", [128, HC * 1024], BF16)
        hch = [ar.alloc("hch0", [128, 1024], F32), ar.alloc("hch1", [128, 1024], F32)]
        hqi = ar.alloc("hqi", [128, 1024], I32)
        ffst = [ar.alloc("ffst0", [128, 1024], F32), ar.alloc("ffst1", [128, 1024], F32)]
        x1pr = [ar.alloc("x1pr0", [128, 1024], F32), ar.alloc("x1pr1", [128, 1024], F32)]

        hq_ld = {}
        hq_q = {}
        f2_mm = {}
        f2_ev = {}
        f2_tt = {}
        f2_sp = {}
        x1p_ld = {}

        def p5_hq_ld(sync, th, oc):
            if (th, oc) == (0, 0):
                sync.wait_ge(sVE.h, state["wq2"])
                sync.wait_ge(sDO.h, sDO.v)
            if th >= 1 and oc == 0:
                sync.wait_ge(sPE.h, f2_mm[(th - 1, 1, 3)])
            prev = (th, oc - 2) if oc >= 2 else (th - 1, HC - 2 + oc) if th >= 1 else None
            if prev in hq_q:
                sync.wait_ge(sVE.h, hq_q[prev])
            sync.dma_start(out=hch[oc % 2][:], in_=h_own[oc][:, th * 1024:(th + 1) * 1024]) \
                .then_inc(sIN.h, 16)
            hq_ld[(th, oc)] = sIN.inc(16)

        def p5_hq_q(v, th, oc):
            v.wait_ge(sIN.h, hq_ld[(th, oc)])
            v.tensor_scalar(out=hqi[:], in0=hch[oc % 2][:], scalar1=sclB[:, 9:10],
                            scalar2=None, op0=ALU.mult)
            v.drain()
            v.tensor_copy(out=hq_bf[:, oc * 1024:(oc + 1) * 1024], in_=hqi[:]) \
                .then_inc(sVE.h, 1)
            hq_q[(th, oc)] = sVE.inc()

        def p5_f2_mm(t, th, dg, d_):
            dc = dg * 4 + d_
            ps = psA if d_ < 2 else psB
            off = (d_ % 2) * 1024
            if (dg, d_) == (0, 0) and th == 0:
                t.wait_ge(sVE.h, state["w2bcc"])
            prev = (th, dg * 4 - 4 + d_) if dg >= 1 else ((th - 1, 4 + d_) if th >= 1 else None)
            if prev in f2_ev:
                t.wait_ge(sSE.h, f2_ev[prev])
            for oc in range(HC):
                t.wait_ge(sVE.h, hq_q[(th, oc)])
                for nn_ in range(2):
                    ins = t.matmul(
                        out=ps[:, off + nn_ * 512: off + (nn_ + 1) * 512],
                        lhsT=wq2_bf[:, oc * D + dc * 128: oc * D + dc * 128 + 128],
                        rhs=hq_bf[:, oc * 1024 + nn_ * 512: oc * 1024 + (nn_ + 1) * 512],
                        start=(oc == 0), stop=(oc == HC - 1))
            ins.then_inc(sPE.h, 1)
            f2_mm[(th, dg, d_)] = sPE.inc()

        f2_ev_last = {}

        def p5_x1p_ld(sync, th, dg, d_):
            dc = dg * 4 + d_
            prev = (th, dc - 2) if dc >= 2 else ((th - 1, DC - 2 + dc) if th >= 1 else None)
            if prev in f2_tt:
                sync.wait_ge(sVE.h, f2_tt[prev])
            sync.dma_start(out=x1pr[dc % 2][:], in_=x1p_hbm[dc][:, th * 1024:(th + 1) * 1024]) \
                .then_inc(sIN.h, 16)
            x1p_ld[(th, dc)] = sIN.inc(16)

        def p5_f2_ev(s, th, dg, d_):
            dc = dg * 4 + d_
            ps = psA if d_ < 2 else psB
            off = (d_ % 2) * 1024
            s.wait_ge(sPE.h, f2_mm[(th, dg, d_)])
            prev = (th, dc - 2) if dc >= 2 else ((th - 1, DC - 2 + dc) if th >= 1 else None)
            if prev in f2_sp:
                s.wait_ge(sDO.h, f2_sp[prev])
            ins = s.activation(out=ffst[dc % 2][:], in_=ps[:, off:off + 1024],
                               func=AF.Identity, scale=sclB[:, 8:9], bias=b2c[:, dc:dc + 1])
            ins.then_inc(sSE.h, 1)
            f2_ev[(th, dc)] = sSE.inc()

        def p5_f2_tt(v, th, dg, d_):
            dc = dg * 4 + d_
            v.wait_ge(sSE.h, f2_ev[(th, dc)])
            v.wait_ge(sIN.h, x1p_ld[(th, dc)])
            v.tensor_tensor(out=ffst[dc % 2][:], in0=ffst[dc % 2][:], in1=x1pr[dc % 2][:],
                            op=ALU.add).then_inc(sVE.h, 1)
            f2_tt[(th, dc)] = sVE.inc()

        def p5_f2_sp(sync, th, dg, d_):
            dc = dg * 4 + d_
            sync.wait_ge(sVE.h, f2_tt[(th, dc)])
            sync.dma_start(out=x1_hbm[dc][:, th * 1024:(th + 1) * 1024],
                           in_=ffst[dc % 2][:]).then_inc(sDO.h, 16)
            f2_sp[(th, dc)] = sDO.inc(16)

        for th in range(2):
            for oc in range(HC):
                block.sync(lambda sync, th=th, oc=oc: p5_hq_ld(sync, th, oc))
                block.vector(lambda v, th=th, oc=oc: p5_hq_q(v, th, oc))
            for dg in range(2):
                for d_ in range(4):
                    block.tensor(lambda t, th=th, dg=dg, d_=d_: p5_f2_mm(t, th, dg, d_))
                    block.sync(lambda sync, th=th, dg=dg, d_=d_: p5_x1p_ld(sync, th, dg, d_))
                    block.scalar(lambda s, th=th, dg=dg, d_=d_: p5_f2_ev(s, th, dg, d_))
                    block.vector(lambda v, th=th, dg=dg, d_=d_: p5_f2_tt(v, th, dg, d_))
                    block.sync(lambda sync, th=th, dg=dg, d_=d_: p5_f2_sp(sync, th, dg, d_))
        state["P5_END"] = sDO.v

        d_strip = dbg_tensor("strip", [128, 64])
        d_w2sl = dbg_tensor("w2sl", [128, 4 * D])
        if d_strip is not None:
            def p5_dbg2(sync):
                sync.wait_ge(sVE.h, state["w2fin"])
                sync.dma_start(out=d_strip[:], in_=strip[:]).then_inc(sDO.h, 16)
                sDO.inc(16)
                sync.dma_start(out=d_w2sl[:], in_=w2sl[0][:]).then_inc(sDO.h, 16)
                sDO.inc(16)
            block.sync(p5_dbg2)

        d_x1 = dbg_tensor("x1", [DC, 128, T])
        d_hq = dbg_tensor("hq", [128, HC * 1024], BF16)
        dbg_tensor("sclB", [128, 16])
        d_scl5 = dbg_tensor("scl5", [1, 64])
        if d_x1 is not None or d_hq is not None or d_scl5 is not None:
            def p5_dbg(sync):
                sync.wait_ge(sDO.h, state["P5_END"])
                sync.wait_ge(sVE.h, sVE.v)
                if d_x1 is not None:
                    for fc in range(DC):
                        sync.dma_start(out=d_x1[fc], in_=x1_hbm[fc]).then_inc(sDO.h, 16)
                        sDO.inc(16)
                if d_hq is not None:
                    sync.dma_start(out=d_hq[:], in_=hq_bf[:]).then_inc(sDO.h, 16)
                    sDO.inc(16)
                if d_scl5 is not None:
                    sync.dma_start(out=d_scl5[:], in_=scl[0:1, 0:64]).then_inc(sDO.h, 16)
                    sDO.inc(16)
                if "sclB" in dbg_out:
                    sync.dma_start(out=dbg_out["sclB"][:], in_=sclB[:]).then_inc(sDO.h, 16)
                    sDO.inc(16)
            block.sync(p5_dbg)

        if phases < 6:
            def p5_out(sync):
                sync.wait_ge(sDO.h, sDO.v)
                sync.dma_start(out=y[0:128, 0:1024], in_=ffst[0][:]).then_inc(sDO.h, 16)
                sDO.inc(16)
                sync.wait_ge(sDO.h, sDO.v)
            block.sync(p5_out)
            return nc, dbg_out

        ar.freebuf("wq2_bf", "hq_bf", "hch0", "hch1", "hqi", "ffst0", "ffst1",
                   "x1pr0", "x1pr1")
        x1ch = [ar.alloc("x1ch0", [128, T], F32), ar.alloc("x1ch1", [128, T], F32)]
        sq6 = ar.alloc("sq6", [128, T], F32)
        rB6 = ar.alloc("rB6", [128, T], F32)
        mB6 = ar.alloc("mB6", [128, T], F32)
        xn1_bf = ar.alloc("xn1_bf", [128, DC * T], BF16)

        x1_ld6 = [None] * DC
        sq6_d = [None] * DC
        ln1_mm = [None] * DC

        def p6_x1_ld(sync, fc):
            if fc == 0:
                sync.wait_ge(sDO.h, state["P5_END"])
                sync.wait_ge(sVE.h, sVE.v)
                sync.wait_ge(sPE.h, sPE.v)
                sync.wait_ge(sSE.h, sSE.v)
            if fc >= 2:
                sync.wait_ge(sPE.h, ln1_mm[fc - 2])
            sync.dma_start(out=x1ch[fc % 2][:], in_=x1_hbm[fc]).then_inc(sIN.h, 16)
            x1_ld6[fc] = sIN.inc(16)

        def p6_sq(s, fc):
            s.wait_ge(sIN.h, x1_ld6[fc])
            if fc >= 1:
                s.wait_ge(sPE.h, ln1_mm[fc - 1])
            s.activation(out=sq6[:], in_=x1ch[fc % 2][:], func=AF.Square).then_inc(sSE.h, 1)
            sq6_d[fc] = sSE.inc()

        def p6_mm(t, fc):
            t.wait_ge(sIN.h, x1_ld6[fc])
            st_ = fc == 0
            sp_ = fc == DC - 1
            for nn_ in range(4):
                t.matmul(out=psB[0:1, nn_ * 512:(nn_ + 1) * 512], lhsT=ones_col[:, 0:1],
                         rhs=x1ch[fc % 2][:, nn_ * 512:(nn_ + 1) * 512], start=st_, stop=sp_)
            t.wait_ge(sSE.h, sq6_d[fc])
            for nn_ in range(4):
                ins = t.matmul(out=psA[0:1, nn_ * 512:(nn_ + 1) * 512], lhsT=ones_col[:, 0:1],
                               rhs=sq6[:, nn_ * 512:(nn_ + 1) * 512], start=st_, stop=sp_)
            ins.then_inc(sPE.h, 1)
            ln1_mm[fc] = sPE.inc()

        for fc in range(DC):
            block.sync(lambda sync, fc=fc: p6_x1_ld(sync, fc))
            block.scalar(lambda s, fc=fc: p6_sq(s, fc))
            block.tensor(lambda t, fc=fc: p6_mm(t, fc))

        def p6_ev(v):
            v.wait_ge(sPE.h, ln1_mm[DC - 1])
            v.tensor_copy(out=srow[0:1, 0:T], in_=psB[0:1, 0:T]).then_inc(sVE.h, 1)
            state["ln1ev1"] = sVE.inc()
        block.vector(p6_ev)

        def p6_rp1(g):
            g.wait_ge(sVE.h, state["ln1ev1"])
            g.dma_start(out=rp[:, 0:16], in_=srow[0:1, 0:T]).then_inc(sGC.h, 16)
            state["ln1rp1"] = sGC.inc(16)
        block.gpsimd(p6_rp1)

        def p6_ev2(v):
            v.wait_ge(sGC.h, state["ln1rp1"])
            v.tensor_copy(out=srow[0:1, 0:T], in_=psA[0:1, 0:T]).then_inc(sVE.h, 1)
            state["ln1ev2"] = sVE.inc()
        block.vector(p6_ev2)

        def p6_rp2(g):
            g.wait_ge(sVE.h, state["ln1ev2"])
            g.dma_start(out=rp[:, 16:32], in_=srow[0:1, 0:T]).then_inc(sGC.h, 16)
            state["ln1rp2"] = sGC.inc(16)
        block.gpsimd(p6_rp2)

        def p6_strip(v):
            v.wait_ge(sGC.h, state["ln1rp2"])
            v.tensor_scalar(out=rp[:, 32:48], in0=rp[:, 0:16], scalar1=-1.0 / D,
                            scalar2=None, op0=ALU.mult)
            v.tensor_scalar(out=rp[:, 16:32], in0=rp[:, 16:32], scalar1=1.0 / D,
                            scalar2=None, op0=ALU.mult)
            v.drain()
            v.tensor_tensor(out=rp[:, 48:64], in0=rp[:, 32:48], in1=rp[:, 32:48], op=ALU.mult)
            v.drain()
            v.tensor_tensor(out=rp[:, 0:16], in0=rp[:, 16:32], in1=rp[:, 48:64], op=ALU.subtract)
            v.drain()
            v.tensor_scalar(out=rp[:, 0:16], in0=rp[:, 0:16], scalar1=EPS, scalar2=None,
                            op0=ALU.add).then_inc(sVE.h, 1)
            state["ln1ms"] = sVE.inc()
        block.vector(p6_strip)

        def p6_sqrt(s):
            s.wait_ge(sVE.h, state["ln1ms"])
            s.sqrt(out=rp[:, 16:32], in_=rp[:, 0:16]).then_inc(sSE.h, 1)
            state["ln1sq"] = sSE.inc()
        block.scalar(p6_sqrt)

        def p6_strip2(v):
            v.wait_ge(sSE.h, state["ln1sq"])
            v.reciprocal(out=rp[:, 0:16], in_=rp[:, 16:32])
            v.drain()
            v.tensor_tensor(out=rp[:, 16:32], in0=rp[:, 32:48], in1=rp[:, 0:16], op=ALU.mult)
            v.drain()
            v.tensor_copy(out=stats_rp2[:, 0:16], in_=rp[:, 0:16])
            v.tensor_copy(out=stats_rp2[:, 16:32], in_=rp[:, 16:32]).then_inc(sVE.h, 1)
            state["ln1st2"] = sVE.inc()
        block.vector(p6_strip2)

        def p6_rp3(g):
            g.wait_ge(sVE.h, state["ln1st2"])
            g.dma_start(out=srow[0:1, 0:T], in_=rp[:, 0:16]).then_inc(sGC.h, 16)
            state["ln1rp3"] = sGC.inc(16)
        block.gpsimd(p6_rp3)

        def p6_bc1(t):
            t.wait_ge(sGC.h, state["ln1rp3"])
            for nn_ in range(4):
                ins = t.matmul(out=psA[:, nn_ * 512:(nn_ + 1) * 512], lhsT=ones_row[0:1, :],
                               rhs=srow[0:1, nn_ * 512:(nn_ + 1) * 512], start=True, stop=True)
            ins.then_inc(sPE.h, 1)
            state["ln1bc1"] = sPE.inc()
        block.tensor(p6_bc1)

        def p6_cp1(v):
            v.wait_ge(sPE.h, state["ln1bc1"])
            v.tensor_copy(out=rB6[:], in_=psA[:, 0:T]).then_inc(sVE.h, 1)
            state["ln1cp1"] = sVE.inc()
        block.vector(p6_cp1)

        def p6_rp4(g):
            g.wait_ge(sVE.h, state["ln1cp1"])
            g.dma_start(out=srow[0:1, 0:T], in_=rp[:, 16:32]).then_inc(sGC.h, 16)
            state["ln1rp4"] = sGC.inc(16)
        block.gpsimd(p6_rp4)

        def p6_bc2(t):
            t.wait_ge(sGC.h, state["ln1rp4"])
            for nn_ in range(4):
                ins = t.matmul(out=psB[:, nn_ * 512:(nn_ + 1) * 512], lhsT=ones_row[0:1, :],
                               rhs=srow[0:1, nn_ * 512:(nn_ + 1) * 512], start=True, stop=True)
            ins.then_inc(sPE.h, 1)
            state["ln1bc2"] = sPE.inc()
        block.tensor(p6_bc2)

        def p6_cp2(v):
            v.wait_ge(sPE.h, state["ln1bc2"])
            v.tensor_copy(out=mB6[:], in_=psB[:, 0:T]).then_inc(sVE.h, 1)
            state["ln1cp2"] = sVE.inc()
        block.vector(p6_cp2)

        ap6_d = [None] * DC
        x1_ld6b = [None] * DC

        def p6_x1_ldb(sync, fc):
            if fc >= 2:
                sync.wait_ge(sVE.h, ap6_d[fc - 2])
            sync.dma_start(out=x1ch[fc % 2][:], in_=x1_hbm[fc]).then_inc(sIN.h, 16)
            x1_ld6b[fc] = sIN.inc(16)

        def p6_apply(v, fc):
            v.wait_ge(sIN.h, x1_ld6b[fc])
            if fc == 0:
                v.wait_ge(sVE.h, state["ln1cp2"])
            v.tensor_tensor(out=x1ch[fc % 2][:], in0=x1ch[fc % 2][:], in1=rB6[:], op=ALU.mult)
            v.drain()
            v.tensor_tensor(out=xn1_bf[:, fc * T:(fc + 1) * T], in0=x1ch[fc % 2][:],
                            in1=mB6[:], op=ALU.add).then_inc(sVE.h, 1)
            ap6_d[fc] = sVE.inc()

        for fc in range(DC):
            block.sync(lambda sync, fc=fc: p6_x1_ldb(sync, fc))
            block.vector(lambda v, fc=fc: p6_apply(v, fc))
        state["XN1"] = ap6_d[DC - 1]

        d_xn1 = dbg_tensor("xn1", [128, DC * T], BF16)
        if d_xn1 is not None:
            def p6_dbg(sync):
                sync.wait_ge(sVE.h, state["XN1"])
                sync.dma_start(out=d_xn1[:], in_=xn1_bf[:]).then_inc(sDO.h, 16)
                sDO.inc(16)
            block.sync(p6_dbg)

        if phases < 7:
            def p6_out(sync):
                sync.wait_ge(sVE.h, state["XN1"])
                sync.dma_start(out=y[0:128, 0:D], in_=rB6[:, 0:D]).then_inc(sDO.h, 16)
                sDO.inc(16)
                sync.wait_ge(sDO.h, sDO.v)
            block.sync(p6_out)
            return nc, dbg_out

        ar.freebuf("x1ch0", "x1ch1", "sq6", "rB6", "mB6")
        cosT_sb = ar.alloc("cosT_sb", [32, T], F32)
        sinTs_sb = ar.alloc("sinTs_sb", [32, T], F32)
        invsq96 = ar.alloc("invsq96", [128, 1], F32)
        Wsm_bf = ar.alloc("Wsm_bf", [128, DC * 128 + 2 * D + 32], BF16)
        OFF_KVD = 0
        OFF_KU = DC * 128
        OFF_VU = OFF_KU + D
        OFF_KR = OFF_VU + D
        ckv_bf = ar.alloc("ckv_bf", [128, T], BF16)
        kT7 = ar.alloc("kT7", [128, DC * T], BF16)
        v7 = ar.alloc("v7", [128, 16 * 1024], BF16)
        kr7 = ar.alloc("kr7", [32, T], BF16)
        qr7 = ar.alloc("qr7", [32, T], BF16)
        rfs = ar.alloc("rfs", [32, T], F32)
        rfswap = ar.alloc("rfswap", [32, T], F32)
        Wqr_sb = ar.alloc("Wqr_sb", [128, DC * 32], BF16)

        def p7_wload(g):
            g.wait_ge(sVE.h, state["XN1"])
            g.dma_start(out=cosT_sb[:], in_=cosT_h.ap()).then_inc(sWT.h, 16)
            g.dma_start(out=sinTs_sb[:], in_=sinTs_h.ap()).then_inc(sWT.h, 16)
            g.dma_start(out=invsq96[:], in_=invsq96_h.ap()).then_inc(sWT.h, 16)
            sWT.inc(48)
            g.dma_start(out=Wsm_bf[:, OFF_KVD:OFF_KU], in_=Wkvd_t[:]).then_inc(sWT.h, 16)
            g.dma_start(out=Wsm_bf[:, OFF_KU:OFF_VU], in_=Wku_in[:]).then_inc(sWT.h, 16)
            g.dma_start(out=Wsm_bf[:, OFF_VU:OFF_KR], in_=Wvu_in[:]).then_inc(sWT.h, 16)
            g.dma_start(out=Wsm_bf[0:128, OFF_KR:OFF_KR + 32], in_=Wkr_in[:]).then_inc(sWT.h, 16)
            g.dma_start(out=Wqr_sb[:], in_=Wqr_t[:]).then_inc(sWT.h, 16)
            sWT.inc(80)
        block.gpsimd(p7_wload)
        state["w7"] = sWT.v

        def p7_ckv_mm(t):
            t.wait_ge(sWT.h, state["w7"])
            t.wait_ge(sVE.h, state["XN1"])
            for ic in range(DC):
                for nn_ in range(4):
                    ins = t.matmul(out=psA[:, nn_ * 512:(nn_ + 1) * 512],
                                   lhsT=Wsm_bf[:, OFF_KVD + ic * 128:OFF_KVD + (ic + 1) * 128],
                                   rhs=xn1_bf[:, ic * T + nn_ * 512: ic * T + (nn_ + 1) * 512],
                                   start=(ic == 0), stop=(ic == DC - 1))
            ins.then_inc(sPE.h, 1)
            state["ckvmm"] = sPE.inc()
        block.tensor(p7_ckv_mm)

        def p7_ckv_ev(s):
            s.wait_ge(sPE.h, state["ckvmm"])
            s.activation(out=ckv_bf[:], in_=psA[:, 0:T], func=AF.Identity).then_inc(sSE.h, 1)
            state["ckvev"] = sSE.inc()
        block.scalar(p7_ckv_ev)

        k7_mm = [None] * DC
        k7_ev = [None] * DC

        def p7_k_mm(t, oc):
            t.wait_ge(sSE.h, state["ckvev"])
            if oc >= 2:
                t.wait_ge(sSE.h, k7_ev[oc - 2])
            ps = psA if oc % 2 == 0 else psB
            for nn_ in range(4):
                ins = t.matmul(out=ps[:, nn_ * 512:(nn_ + 1) * 512],
                               lhsT=Wsm_bf[:, OFF_KU + oc * 128:OFF_KU + (oc + 1) * 128],
                               rhs=ckv_bf[:, nn_ * 512:(nn_ + 1) * 512],
                               start=True, stop=True)
            ins.then_inc(sPE.h, 1)
            k7_mm[oc] = sPE.inc()

        def p7_k_ev(s, oc):
            ps = psA if oc % 2 == 0 else psB
            s.wait_ge(sPE.h, k7_mm[oc])
            s.activation(out=kT7[:, oc * T:(oc + 1) * T], in_=ps[:, 0:T],
                         func=AF.Identity).then_inc(sSE.h, 1)
            k7_ev[oc] = sSE.inc()

        for oc in range(DC):
            block.tensor(lambda t, oc=oc: p7_k_mm(t, oc))
            block.scalar(lambda s, oc=oc: p7_k_ev(s, oc))

        v7_mm = [None] * 16
        v7_ev = [None] * 16

        def p7_v_mm(t, tc):
            if tc < 2:
                t.wait_ge(sSE.h, k7_ev[DC - 2 + tc])
            if tc >= 2:
                t.wait_ge(sSE.h, v7_ev[tc - 2])
            ps = psA if tc % 2 == 0 else psB
            for nn_ in range(2):
                ins = t.matmul(out=ps[:, nn_ * 512:(nn_ + 1) * 512],
                               lhsT=ckv_bf[:, tc * 128:(tc + 1) * 128],
                               rhs=Wsm_bf[:, OFF_VU + nn_ * 512:OFF_VU + (nn_ + 1) * 512],
                               start=True, stop=True)
            ins.then_inc(sPE.h, 1)
            v7_mm[tc] = sPE.inc()

        def p7_v_ev(s, tc):
            ps = psA if tc % 2 == 0 else psB
            s.wait_ge(sPE.h, v7_mm[tc])
            s.activation(out=v7[:, tc * 1024:(tc + 1) * 1024], in_=ps[:, 0:1024],
                         func=AF.Identity).then_inc(sSE.h, 1)
            v7_ev[tc] = sSE.inc()

        for tc in range(16):
            block.tensor(lambda t, tc=tc: p7_v_mm(t, tc))
            block.scalar(lambda s, tc=tc: p7_v_ev(s, tc))

        def p7_kr_mm(t):
            t.wait_ge(sSE.h, v7_ev[15])
            for nn_ in range(4):
                ins = t.matmul(out=psA[0:32, nn_ * 512:(nn_ + 1) * 512],
                               lhsT=Wsm_bf[:, OFF_KR:OFF_KR + 32],
                               rhs=ckv_bf[:, nn_ * 512:(nn_ + 1) * 512],
                               start=True, stop=True)
            ins.then_inc(sPE.h, 1)
            state["krmm"] = sPE.inc()
        block.tensor(p7_kr_mm)

        def p7_kr_ev(s):
            s.wait_ge(sPE.h, state["krmm"])
            s.activation(out=rfs[:], in_=psA[0:32, 0:T], func=AF.Identity).then_inc(sSE.h, 1)
            state["krev"] = sSE.inc()
        block.scalar(p7_kr_ev)

        def p7_kr_swp(g):
            g.wait_ge(sSE.h, state["krev"])
            g.dma_start(out=rfswap[0:16, :], in_=rfs[16:32, :]).then_inc(sGC.h, 16)
            g.dma_start(out=rfswap[16:32, :], in_=rfs[0:16, :]).then_inc(sGC.h, 16)
            state["krswp"] = sGC.inc(32)
        block.gpsimd(p7_kr_swp)

        def p7_kr_rot(v):
            v.wait_ge(sGC.h, state["krswp"])
            v.tensor_tensor(out=rfs[:], in0=rfs[:], in1=cosT_sb[:], op=ALU.mult)
            v.tensor_tensor(out=rfswap[:], in0=rfswap[:], in1=sinTs_sb[:], op=ALU.mult)
            v.drain()
            v.tensor_tensor(out=kr7[:], in0=rfs[:], in1=rfswap[:], op=ALU.add) \
                .then_inc(sVE.h, 1)
            state["krrot"] = sVE.inc()
        block.vector(p7_kr_rot)

        def p7_qr_mm(t):
            t.wait_ge(sVE.h, state["krrot"])
            for ic in range(DC):
                for nn_ in range(4):
                    ins = t.matmul(out=psA[0:32, nn_ * 512:(nn_ + 1) * 512],
                                   lhsT=Wqr_sb[:, ic * 32:(ic + 1) * 32],
                                   rhs=xn1_bf[:, ic * T + nn_ * 512: ic * T + (nn_ + 1) * 512],
                                   start=(ic == 0), stop=(ic == DC - 1))
            ins.then_inc(sPE.h, 1)
            state["qrmm"] = sPE.inc()
        block.tensor(p7_qr_mm)

        def p7_qr_ev(s):
            s.wait_ge(sPE.h, state["qrmm"])
            s.activation(out=rfs[:], in_=psA[0:32, 0:T], func=AF.Identity).then_inc(sSE.h, 1)
            state["qrev"] = sSE.inc()
        block.scalar(p7_qr_ev)

        def p7_qr_swp(g):
            g.wait_ge(sSE.h, state["qrev"])
            g.dma_start(out=rfswap[0:16, :], in_=rfs[16:32, :]).then_inc(sGC.h, 16)
            g.dma_start(out=rfswap[16:32, :], in_=rfs[0:16, :]).then_inc(sGC.h, 16)
            state["qrswp"] = sGC.inc(32)
        block.gpsimd(p7_qr_swp)

        def p7_qr_rot(v):
            v.wait_ge(sGC.h, state["qrswp"])
            v.tensor_tensor(out=rfs[:], in0=rfs[:], in1=cosT_sb[:], op=ALU.mult)
            v.tensor_tensor(out=rfswap[:], in0=rfswap[:], in1=sinTs_sb[:], op=ALU.mult)
            v.drain()
            v.tensor_tensor(out=qr7[:], in0=rfs[:], in1=rfswap[:], op=ALU.add) \
                .then_inc(sVE.h, 1)
            state["qrrot"] = sVE.inc()
        block.vector(p7_qr_rot)

        ar.freebuf("ckv_bf", "Wsm_bf", "srow", "cosT_sb", "sinTs_sb", "rfs",
                   "rfswap", "Wqr_sb")
        qT7 = ar.alloc("qT7", [128, DC * T], BF16)
        Wq1_bf = ar.alloc("Wq1_bf", [128, DC * D], BF16)

        def p7_wload2(g):
            g.wait_ge(sVE.h, state["qrrot"])
            g.dma_start(out=Wq1_bf[:], in_=Wq1_t[:]).then_inc(sWT.h, 16)
            sWT.inc(16)
        block.gpsimd(p7_wload2)
        state["w7b"] = sWT.v

        q7_mm = [None] * DC
        q7_ev = [None] * DC

        def p7_q_mm(t, oc):
            if oc == 0:
                t.wait_ge(sWT.h, state["w7b"])
            if oc >= 2:
                t.wait_ge(sSE.h, q7_ev[oc - 2])
            ps = psA if oc % 2 == 0 else psB
            for ic in range(DC):
                for nn_ in range(4):
                    ins = t.matmul(out=ps[:, nn_ * 512:(nn_ + 1) * 512],
                                   lhsT=Wq1_bf[:, ic * D + oc * 128: ic * D + oc * 128 + 128],
                                   rhs=xn1_bf[:, ic * T + nn_ * 512: ic * T + (nn_ + 1) * 512],
                                   start=(ic == 0), stop=(ic == DC - 1))
            ins.then_inc(sPE.h, 1)
            q7_mm[oc] = sPE.inc()

        def p7_q_ev(s, oc):
            ps = psA if oc % 2 == 0 else psB
            s.wait_ge(sPE.h, q7_mm[oc])
            s.activation(out=qT7[:, oc * T:(oc + 1) * T], in_=ps[:, 0:T],
                         func=AF.Identity).then_inc(sSE.h, 1)
            q7_ev[oc] = sSE.inc()

        for oc in range(DC):
            block.tensor(lambda t, oc=oc: p7_q_mm(t, oc))
            block.scalar(lambda s, oc=oc: p7_q_ev(s, oc))
        state["qdone"] = q7_ev[DC - 1]

        if False:
            def p7_stop1(sync):
                sync.wait_ge(sSE.h, state["qdone"])
                sync.wait_ge(sVE.h, sVE.v)
                sync.dma_start(out=y[0:32, 0:T // 2], in_=rfs[0:32, 0:T // 2]).then_inc(sDO.h, 16)
                sDO.inc(16)
                sync.wait_ge(sDO.h, sDO.v)
            block.sync(p7_stop1)
            return nc, dbg_out

        ar.freebuf("xn1_bf")
        attnT_bf = ar.alloc("attnT_bf", [128, DC * T], BF16)
        P_sb = ar.alloc("P_sb", [128, T], F32)
        PT_bf = ar.alloc("PT_bf", [128, T], BF16)
        pstage = ar.alloc("pstage", [128, 128], F32)

        fl_s = {}
        fl_m = {}
        fl_e = {}
        fl_n = {}
        fl_t = {}
        fl_pv = {}
        fl_pe = {}
        fl_pt = {}
        fl_pc = {}
        tr_done = {}
        te_done = {}
        pc_hist = {}
        pt_last_reader = {}

        flash_keys = [(op, qt, hh) for op in range(8) for qt in range(16) for hh in range(2)]
        prev_flash = {}
        for i_, k_ in enumerate(flash_keys):
            prev_flash[k_] = flash_keys[i_ - 1] if i_ >= 1 else None

        def p7_fl_scores(t, op, qt, hh):
            pr = hh * 64
            kl = 128 * (qt + 1)
            key = (op, qt, hh)
            prevk = prev_flash[key]
            if prevk is None:
                t.wait_ge(sSE.h, state["qdone"])
            else:
                t.wait_ge(sSE.h, fl_e[prevk])
            for n0 in range(0, kl, 512):
                nn = min(512, kl - n0)
                t.matmul(out=psA[:, n0:n0 + nn],
                         lhsT=qT7[pr:pr + 64, op * T + qt * 128: op * T + (qt + 1) * 128],
                         rhs=kT7[pr:pr + 64, op * T + n0: op * T + n0 + nn],
                         start=True, stop=False)
            for n0 in range(0, kl, 512):
                nn = min(512, kl - n0)
                ins = t.matmul(out=psA[:, n0:n0 + nn],
                               lhsT=qr7[0:32, qt * 128:(qt + 1) * 128],
                               rhs=kr7[0:32, n0:n0 + nn],
                               start=False, stop=True)
            ins.then_inc(sPE.h, 1)
            fl_s[key] = sPE.inc()

        def p7_fl_mask(v, op, qt, hh):
            key = (op, qt, hh)
            kl = 128 * (qt + 1)
            v.wait_ge(sPE.h, fl_s[key])
            v.tensor_tensor(out=psA[:, qt * 128:(qt + 1) * 128],
                            in0=psA[:, qt * 128:(qt + 1) * 128], in1=mask_sb[:], op=ALU.add)
            v.drain()
            v.tensor_reduce(out=strip[:, 0:1], in_=psA[:, 0:min(kl, 1024)],
                            axis=AX.X, op=ALU.max)
            if kl > 1024:
                v.tensor_reduce(out=strip[:, 1:2], in_=psA[:, 1024:kl],
                                axis=AX.X, op=ALU.max)
                v.drain()
                v.tensor_tensor(out=strip[:, 0:1], in0=strip[:, 0:1], in1=strip[:, 1:2],
                                op=ALU.max)
            v.drain()
            v.tensor_scalar(out=strip[:, 2:3], in0=strip[:, 0:1],
                            scalar1=-1.0 / math.sqrt(96.0), scalar2=None, op0=ALU.mult) \
                .then_inc(sVE.h, 1)
            fl_m[key] = sVE.inc()

        def p7_fl_exp(s, op, qt, hh):
            key = (op, qt, hh)
            kl = 128 * (qt + 1)
            s.wait_ge(sVE.h, fl_m[key])
            prevk = prev_flash[key]
            if prevk is not None:
                s.wait_ge(sPE.h, fl_t[prevk])
            s.activation(out=P_sb[:, 0:min(kl, 1024)], in_=psA[:, 0:min(kl, 1024)],
                         func=AF.Exp, scale=invsq96[:, 0:1], bias=strip[:, 2:3],
                         accum_out=strip[:, 3:4])
            if kl > 1024:
                ins = s.activation(out=P_sb[:, 1024:kl], in_=psA[:, 1024:kl],
                                   func=AF.Exp, scale=invsq96[:, 0:1], bias=strip[:, 2:3],
                                   accum_out=strip[:, 4:5])
            else:
                ins = s.activation(out=strip[:, 4:5], in_=strip[:, 63:64], func=AF.Identity)
            ins.then_inc(sSE.h, 1)
            fl_e[key] = sSE.inc()

        def p7_fl_norm(v, op, qt, hh):
            key = (op, qt, hh)
            kl = 128 * (qt + 1)
            v.wait_ge(sSE.h, fl_e[key])
            if kl > 1024:
                v.tensor_tensor(out=strip[:, 3:4], in0=strip[:, 3:4], in1=strip[:, 4:5],
                                op=ALU.add)
                v.drain()
            v.reciprocal(out=strip[:, 6:7], in_=strip[:, 3:4])
            v.drain()
            v.tensor_scalar(out=P_sb[:, 0:kl], in0=P_sb[:, 0:kl], scalar1=strip[:, 6:7],
                            scalar2=None, op0=ALU.mult).then_inc(sVE.h, 1)
            fl_n[key] = sVE.inc()

        def p7_fl_tr_b(t, op, qt, hh, b):
            key = (op, qt, hh)
            kl = 128 * (qt + 1)
            nb = kl // 128
            b0 = b * 4
            bk = b % 2
            if b == 0:
                t.wait_ge(sVE.h, fl_n[key])
            if bk in pt_last_reader:
                t.wait_ge(sSE.h, pt_last_reader[bk])
            for bi in range(b0, min(b0 + 4, nb)):
                ins = t.transpose(out=psB[:, bk * 512 + (bi % 4) * 128:
                                          bk * 512 + (bi % 4) * 128 + 128],
                                  in_=P_sb[:, bi * 128:(bi + 1) * 128],
                                  identity=ident_f[:])
            ins.then_inc(sPE.h, 1)
            tr_done[(key, b)] = sPE.inc()
            fl_t[key] = sPE.v

        def p7_fl_te_b(s, op, qt, hh, b):
            key = (op, qt, hh)
            kl = 128 * (qt + 1)
            nb = kl // 128
            b0 = b * 4
            bk = b % 2
            s.wait_ge(sPE.h, tr_done[(key, b)])
            cnt = min(4, nb - b0)
            ins = s.activation(out=PT_bf[:, b0 * 128:(b0 + cnt) * 128],
                               in_=psB[:, bk * 512: bk * 512 + cnt * 128],
                               func=AF.Identity)
            ins.then_inc(sSE.h, 1)
            te_done[(key, b)] = sSE.inc()
            pt_last_reader[bk] = te_done[(key, b)]

        def p7_fl_pv(t, op, qt, hh):
            key = (op, qt, hh)
            h = 2 * op + hh
            kl = 128 * (qt + 1)
            nb = kl // 128
            base = 1024 if hh == 0 else 1536
            prev_pair = (op, qt - 1) if qt >= 1 else ((op - 1, 15) if op >= 1 else None)
            if prev_pair in fl_pe:
                t.wait_ge(sSE.h, fl_pe[prev_pair])
            if hh == 1 and prev_pair in fl_pc:
                t.wait_ge(sSE.h, fl_pc[prev_pair])
            for kc in range(nb):
                t.wait_ge(sSE.h, te_done[(key, kc // 4)])
                ins = t.matmul(out=psB[:, base: base + 64],
                               lhsT=PT_bf[:, kc * 128:(kc + 1) * 128],
                               rhs=v7[:, kc * 1024 + h * 64: kc * 1024 + h * 64 + 64],
                               start=(kc == 0), stop=(kc == nb - 1))
            ins.then_inc(sPE.h, 1)
            fl_pv[key] = sPE.inc()

        def p7_fl_pairev(s, op, qt):
            s.wait_ge(sPE.h, fl_pv[(op, qt, 1)])
            s.activation(out=pstage[:, 0:64], in_=psB[:, 1024:1088], func=AF.Identity)
            s.activation(out=pstage[:, 64:128], in_=psB[:, 1536:1600], func=AF.Identity) \
                .then_inc(sSE.h, 1)
            fl_pe[(op, qt)] = sSE.inc()

        def p7_fl_pairtr(t, op, qt):
            t.wait_ge(sSE.h, fl_pe[(op, qt)])
            if (op, qt) in pc_hist:
                t.wait_ge(sSE.h, pc_hist[(op, qt)])
            t.transpose(out=psB[:, 1664:1792], in_=pstage[:], identity=ident_f[:]) \
                .then_inc(sPE.h, 1)
            fl_pt[(op, qt)] = sPE.inc()

        def p7_fl_paircast(s, op, qt):
            s.wait_ge(sPE.h, fl_pt[(op, qt)])
            s.activation(out=attnT_bf[:, op * T + qt * 128: op * T + (qt + 1) * 128],
                         in_=psB[:, 1664:1792], func=AF.Identity).then_inc(sSE.h, 1)
            fl_pc[(op, qt)] = sSE.inc()

        N_OPS = 8
        flash_keys = [(op, qt, hh) for op in range(N_OPS) for qt in range(16) for hh in range(2)]
        prev_flash = {}
        for i_, k_ in enumerate(flash_keys):
            prev_flash[k_] = flash_keys[i_ - 1] if i_ >= 1 else None
        LVL = 6

        def p7_fl_dummy_e(s, op, qt, hh):
            key = (op, qt, hh)
            s.wait_ge(sVE.h, fl_m[key]) if LVL >= 2 else s.wait_ge(sPE.h, fl_s[key])
            s.nop().then_inc(sSE.h, 1)
            fl_e[key] = sSE.inc()

        for op in range(N_OPS):
            for qt in range(16):
                for hh in range(2):
                    block.tensor(lambda t, op=op, qt=qt, hh=hh: p7_fl_scores(t, op, qt, hh))
                    if LVL >= 2:
                        block.vector(lambda v, op=op, qt=qt, hh=hh: p7_fl_mask(v, op, qt, hh))
                    if LVL >= 3:
                        block.scalar(lambda s, op=op, qt=qt, hh=hh: p7_fl_exp(s, op, qt, hh))
                    else:
                        block.scalar(lambda s, op=op, qt=qt, hh=hh: p7_fl_dummy_e(s, op, qt, hh))
                        fl_t[(op, qt, hh)] = sPE.v
                        continue
                    if LVL >= 4:
                        block.vector(lambda v, op=op, qt=qt, hh=hh: p7_fl_norm(v, op, qt, hh))
                    else:
                        fl_n[(op, qt, hh)] = sVE.v
                        fl_t[(op, qt, hh)] = sPE.v
                        continue
                    if LVL >= 5:
                        for b in range((qt + 1 + 3) // 4):
                            block.tensor(lambda t, op=op, qt=qt, hh=hh, b=b:
                                         p7_fl_tr_b(t, op, qt, hh, b))
                            block.scalar(lambda s, op=op, qt=qt, hh=hh, b=b:
                                         p7_fl_te_b(s, op, qt, hh, b))
                    else:
                        fl_t[(op, qt, hh)] = sPE.v
                        continue
                    if LVL >= 6:
                        block.tensor(lambda t, op=op, qt=qt, hh=hh: p7_fl_pv(t, op, qt, hh))
                if LVL >= 6:
                    block.scalar(lambda s, op=op, qt=qt: p7_fl_pairev(s, op, qt))
                    block.tensor(lambda t, op=op, qt=qt: p7_fl_pairtr(t, op, qt))
                    block.scalar(lambda s, op=op, qt=qt: p7_fl_paircast(s, op, qt))
        if LVL >= 6:
            state["FLASH"] = fl_pc[(N_OPS - 1, 15)]
        else:
            state["FLASH"] = sSE.v

        dbg_tensor("attnT", [128, DC * T], BF16)
        dbg_tensor("rope", [64, T], BF16)
        if False:
            def p7_stop2(sync):
                sync.wait_ge(sSE.h, state["FLASH"])
                if "attnT" in dbg_out:
                    sync.dma_start(out=dbg_out["attnT"][:], in_=attnT_bf[:]).then_inc(sDO.h, 16)
                    sDO.inc(16)
                if "rope" in dbg_out:
                    sync.dma_start(out=dbg_out["rope"][0:32, :], in_=kr7[:]).then_inc(sDO.h, 16)
                    sync.dma_start(out=dbg_out["rope"][32:64, :], in_=qr7[:]).then_inc(sDO.h, 16)
                    sDO.inc(32)
                sync.dma_start(out=y[0:128, 0:256], in_=attnT_bf[:, 0:512].bitcast(F32)) \
                    .then_inc(sDO.h, 16)
                sDO.inc(16)
                sync.wait_ge(sDO.h, sDO.v)
            block.sync(p7_stop2)
            return nc, dbg_out

        ar.freebuf("P_sb", "PT_bf", "kT7", "v7", "qT7", "Wq1_bf",
                   "kr7", "qr7", "pstage")
        Wo_bf = ar.alloc("Wo_bf", [128, DC * D], BF16)
        x1ch7 = [ar.alloc("x1ch70", [128, T], F32), ar.alloc("x1ch71", [128, T], F32)]
        x1w = ar.alloc("x1w", [128, T], F32)
        x1t_st = ar.alloc("x1t_st", [128, 16 * 128], F32)

        def p7_wo_ld(g):
            g.wait_ge(sSE.h, state["FLASH"])
            g.wait_ge(sPE.h, sPE.v)
            g.wait_ge(sVE.h, sVE.v)
            g.dma_start(out=Wo_bf[:], in_=Wo_t[:]).then_inc(sWT.h, 16)
            sWT.inc(16)
        block.gpsimd(p7_wo_ld)
        state["wo"] = sWT.v

        wo_mm = [None] * DC
        wo_tt = [None] * DC
        wo_tr = {}
        wo_tre = {}
        wo_sp = [None] * DC
        x1_ld7 = [None] * DC

        def p7_wo_x1ld(sync, oc):
            if oc == 0:
                sync.wait_ge(sSE.h, state["FLASH"])
            if oc >= 2:
                sync.wait_ge(sVE.h, wo_tt[oc - 2])
            sync.dma_start(out=x1ch7[oc % 2][:], in_=x1_hbm[oc]).then_inc(sIN.h, 16)
            x1_ld7[oc] = sIN.inc(16)

        def p7_wo_mm(t, oc):
            if oc == 0:
                t.wait_ge(sWT.h, state["wo"])
            if oc >= 1:
                t.wait_ge(sVE.h, wo_tt[oc - 1])
            for ic in range(DC):
                for nn_ in range(4):
                    ins = t.matmul(out=psA[:, nn_ * 512:(nn_ + 1) * 512],
                                   lhsT=Wo_bf[:, ic * D + oc * 128: ic * D + oc * 128 + 128],
                                   rhs=attnT_bf[:, ic * T + nn_ * 512: ic * T + (nn_ + 1) * 512],
                                   start=(ic == 0), stop=(ic == DC - 1))
            ins.then_inc(sPE.h, 1)
            wo_mm[oc] = sPE.inc()

        def p7_wo_tt(v, oc):
            v.wait_ge(sPE.h, wo_mm[oc])
            v.wait_ge(sIN.h, x1_ld7[oc])
            v.tensor_tensor(out=x1w[:], in0=psA[:, 0:T], in1=x1ch7[oc % 2][:],
                            op=ALU.add).then_inc(sVE.h, 1)
            wo_tt[oc] = sVE.inc()

        def p7_wo_tr(t, oc, b0):
            bk = (b0 // 4) % 2
            t.wait_ge(sVE.h, wo_tt[oc])
            prev = (oc, b0 - 8) if b0 >= 8 else ((oc - 1, b0 + 8) if oc >= 1 else None)
            if prev in wo_tre:
                t.wait_ge(sVE.h, wo_tre[prev])
            for bi in range(b0, b0 + 4):
                ins = t.transpose(out=psB[:, bk * 512 + (bi % 4) * 128:
                                          bk * 512 + (bi % 4) * 128 + 128],
                                  in_=x1w[:, bi * 128:(bi + 1) * 128], identity=ident_f[:])
            ins.then_inc(sPE.h, 1)
            wo_tr[(oc, b0)] = sPE.inc()

        def p7_wo_tre(v, oc, b0):
            bk = (b0 // 4) % 2
            v.wait_ge(sPE.h, wo_tr[(oc, b0)])
            v.tensor_copy(out=x1t_st[:, b0 * 128:(b0 + 4) * 128],
                          in_=psB[:, bk * 512: bk * 512 + 512]).then_inc(sVE.h, 1)
            wo_tre[(oc, b0)] = sVE.inc()

        def p7_wo_sp(sync, oc):
            sync.wait_ge(sVE.h, wo_tre[(oc, 12)])
            dst = x1tok_hbm[:, :, oc * 128:(oc + 1) * 128].rearrange("a p c -> p a c")
            src = x1t_st.rearrange("p (a c) -> p a c", c=128)
            sync.dma_start(out=dst, in_=src).then_inc(sDO.h, 16)
            wo_sp[oc] = sDO.inc(16)

        for oc in range(DC):
            block.sync(lambda sync, oc=oc: p7_wo_x1ld(sync, oc))
            block.tensor(lambda t, oc=oc: p7_wo_mm(t, oc))
            block.vector(lambda v, oc=oc: p7_wo_tt(v, oc))
            for b0 in (0, 4, 8, 12):
                block.tensor(lambda t, oc=oc, b0=b0: p7_wo_tr(t, oc, b0))
                block.vector(lambda v, oc=oc, b0=b0: p7_wo_tre(v, oc, b0))
            block.sync(lambda sync, oc=oc: p7_wo_sp(sync, oc))
        state["P7_END"] = wo_sp[DC - 1]

        d_attnT = dbg_out.get("attnT")
        d_rope = dbg_out.get("rope")
        if d_attnT is not None:
            def p7_dbg0(sync):
                sync.wait_ge(sSE.h, state["FLASH"])
                sync.dma_start(out=d_attnT[:], in_=attnT_bf[:]).then_inc(sDO.h, 16)
                sDO.inc(16)
                if d_rope is not None:
                    sync.dma_start(out=d_rope[0:32, :], in_=kr7[:]).then_inc(sDO.h, 16)
                    sync.dma_start(out=d_rope[32:64, :], in_=qr7[:]).then_inc(sDO.h, 16)
                    sDO.inc(32)
            block.sync(p7_dbg0)

        d_x1tok = dbg_tensor("x1tok", [16, 128, D])
        if d_x1tok is not None:
            def p7_dbg(sync):
                sync.wait_ge(sDO.h, state["P7_END"])
                for tc in range(16):
                    sync.dma_start(out=d_x1tok[tc], in_=x1tok_hbm[tc]).then_inc(sDO.h, 16)
                    sDO.inc(16)
            block.sync(p7_dbg)

        if phases < 8:
            def p7_out(sync):
                sync.wait_ge(sDO.h, sDO.v)
                sync.dma_start(out=y[0:128, 0:D], in_=x1t_st[:, 0:D]).then_inc(sDO.h, 16)
                sDO.inc(16)
                sync.wait_ge(sDO.h, sDO.v)
            block.sync(p7_out)
            return nc, dbg_out

        ar.freebuf("attnT_bf", "Wo_bf", "x1w", "x1ch70", "x1ch71", "x1t_st",
                   "phiW_sb", "stats_rp", "mxg", "mask_sb", "sel_sb", "bqkv_sb",
                   "b1_sb", "b2_sb", "b2c", "b1c", "strip", "scl", "sclB",
                   "ident_b", "ident_f", "ones_col", "rp", "invsq96")
        xn1b = ar.alloc("xn1b", [128, DC * T], BF16)
        srow8 = ar.alloc("srow8", [1, 2048], F32)
        rB8 = ar.alloc("rB8", [128, T], F32)
        mB8 = ar.alloc("mB8", [128, T], F32)
        x1c8 = [ar.alloc("x1c80", [128, T], F32), ar.alloc("x1c81", [128, T], F32)]

        def p8_rp1(g):
            g.wait_ge(sDO.h, state["P7_END"])
            g.dma_start(out=srow8[0:1, 0:T], in_=stats_rp2[:, 0:16]).then_inc(sGC.h, 16)
            state["p8rp1"] = sGC.inc(16)
        block.gpsimd(p8_rp1)

        def p8_bc1(t):
            t.wait_ge(sGC.h, state["p8rp1"])
            for nn_ in range(4):
                ins = t.matmul(out=psA[:, nn_ * 512:(nn_ + 1) * 512], lhsT=ones_row[0:1, :],
                               rhs=srow8[0:1, nn_ * 512:(nn_ + 1) * 512], start=True, stop=True)
            ins.then_inc(sPE.h, 1)
            state["p8bc1"] = sPE.inc()
        block.tensor(p8_bc1)

        def p8_cp1(v):
            v.wait_ge(sPE.h, state["p8bc1"])
            v.tensor_copy(out=rB8[:], in_=psA[:, 0:T]).then_inc(sVE.h, 1)
            state["p8cp1"] = sVE.inc()
        block.vector(p8_cp1)

        def p8_rp2(g):
            g.wait_ge(sVE.h, state["p8cp1"])
            g.dma_start(out=srow8[0:1, 0:T], in_=stats_rp2[:, 16:32]).then_inc(sGC.h, 16)
            state["p8rp2"] = sGC.inc(16)
        block.gpsimd(p8_rp2)

        def p8_bc2(t):
            t.wait_ge(sGC.h, state["p8rp2"])
            for nn_ in range(4):
                ins = t.matmul(out=psB[:, nn_ * 512:(nn_ + 1) * 512], lhsT=ones_row[0:1, :],
                               rhs=srow8[0:1, nn_ * 512:(nn_ + 1) * 512], start=True, stop=True)
            ins.then_inc(sPE.h, 1)
            state["p8bc2"] = sPE.inc()
        block.tensor(p8_bc2)

        def p8_cp2(v):
            v.wait_ge(sPE.h, state["p8bc2"])
            v.tensor_copy(out=mB8[:], in_=psB[:, 0:T]).then_inc(sVE.h, 1)
            state["p8cp2"] = sVE.inc()
        block.vector(p8_cp2)

        ap8_d = [None] * DC
        x1_ld8 = [None] * DC

        def p8_x1_ld(sync, fc):
            if fc == 0:
                sync.wait_ge(sDO.h, state["P7_END"])
            if fc >= 2:
                sync.wait_ge(sVE.h, ap8_d[fc - 2])
            sync.dma_start(out=x1c8[fc % 2][:], in_=x1_hbm[fc]).then_inc(sIN.h, 16)
            x1_ld8[fc] = sIN.inc(16)

        def p8_apply(v, fc):
            v.wait_ge(sIN.h, x1_ld8[fc])
            if fc == 0:
                v.wait_ge(sVE.h, state["p8cp2"])
            v.tensor_tensor(out=x1c8[fc % 2][:], in0=x1c8[fc % 2][:], in1=rB8[:],
                            op=ALU.mult)
            v.drain()
            v.tensor_tensor(out=xn1b[:, fc * T:(fc + 1) * T], in0=x1c8[fc % 2][:],
                            in1=mB8[:], op=ALU.add).then_inc(sVE.h, 1)
            ap8_d[fc] = sVE.inc()

        for fc in range(DC):
            block.sync(lambda sync, fc=fc: p8_x1_ld(sync, fc))
            block.vector(lambda v, fc=fc: p8_apply(v, fc))
        state["XN1B"] = ap8_d[DC - 1]

        ar.freebuf("x1c80", "x1c81", "rB8", "mB8")
        hL1 = ar.alloc("hL1", [128, HC * T], BF16)
        w1sl8 = [ar.alloc("w1sl80", [128, DC * 128], BF16),
                 ar.alloc("w1sl81", [128, DC * 128], BF16)]
        w2sl8 = [ar.alloc("w2sl80", [128, DC * 128], BF16),
                 ar.alloc("w2sl81", [128, DC * 128], BF16)]
        t1s8 = ar.alloc("t1s8", [128, T], F32)
        t2s8 = ar.alloc("t2s8", [128, T], F32)

        s1_wld = {}
        s1_t1 = {}
        s1_t2 = {}
        s1_ev = {}
        s1_h = {}

        def p8_s1_wld(g, oc):
            if oc == 0:
                g.wait_ge(sVE.h, state["XN1B"])
            if oc >= 2:
                g.wait_ge(sPE.h, s1_t2[oc - 2])
            g.dma_start(out=w1sl8[oc % 2][:], in_=l1w1_t[oc]).then_inc(sWT.h, 16)
            g.dma_start(out=w2sl8[oc % 2][:], in_=l1w2_t[oc]).then_inc(sWT.h, 16)
            s1_wld[oc] = sWT.inc(32)

        def p8_s1_t1(t, oc):
            t.wait_ge(sWT.h, s1_wld[oc])
            if oc == 0:
                t.wait_ge(sVE.h, state["XN1B"])
            if oc >= 1:
                t.wait_ge(sSE.h, s1_ev[oc - 1])
            for ic in range(DC):
                for nn_ in range(4):
                    ins = t.matmul(out=psA[:, nn_ * 512:(nn_ + 1) * 512],
                                   lhsT=w1sl8[oc % 2][:, ic * 128:(ic + 1) * 128],
                                   rhs=xn1b[:, ic * T + nn_ * 512: ic * T + (nn_ + 1) * 512],
                                   start=(ic == 0), stop=(ic == DC - 1))
            ins.then_inc(sPE.h, 1)
            s1_t1[oc] = sPE.inc()

        def p8_s1_t2(t, oc):
            for ic in range(DC):
                for nn_ in range(4):
                    ins = t.matmul(out=psB[:, nn_ * 512:(nn_ + 1) * 512],
                                   lhsT=w2sl8[oc % 2][:, ic * 128:(ic + 1) * 128],
                                   rhs=xn1b[:, ic * T + nn_ * 512: ic * T + (nn_ + 1) * 512],
                                   start=(ic == 0), stop=(ic == DC - 1))
            ins.then_inc(sPE.h, 1)
            s1_t2[oc] = sPE.inc()

        def p8_s1_ev(s, oc):
            s.wait_ge(sPE.h, s1_t1[oc])
            if oc >= 1:
                s.wait_ge(sVE.h, s1_h[oc - 1])
            s.activation(out=t1s8[:], in_=psA[:, 0:T], func=AF.Silu,
                         scale=l1s_sb[:, oc:oc + 1])
            s.wait_ge(sPE.h, s1_t2[oc])
            s.activation(out=t2s8[:], in_=psB[:, 0:T], func=AF.Identity,
                         scale=l1s_sb[:, HC + oc:HC + oc + 1]).then_inc(sSE.h, 1)
            s1_ev[oc] = sSE.inc()

        def p8_s1_h(v, oc):
            v.wait_ge(sSE.h, s1_ev[oc])
            v.tensor_tensor(out=hL1[:, oc * T:(oc + 1) * T], in0=t1s8[:], in1=t2s8[:],
                            op=ALU.mult).then_inc(sVE.h, 1)
            s1_h[oc] = sVE.inc()

        for oc in range(HC):
            block.gpsimd(lambda g, oc=oc: p8_s1_wld(g, oc))
            block.tensor(lambda t, oc=oc: p8_s1_t1(t, oc))
            block.tensor(lambda t, oc=oc: p8_s1_t2(t, oc))
            block.scalar(lambda s, oc=oc: p8_s1_ev(s, oc))
            block.vector(lambda v, oc=oc: p8_s1_h(v, oc))
        state["S1"] = s1_h[HC - 1]

        d_hL1 = dbg_tensor("hL1", [128, HC * T], BF16)
        if d_hL1 is not None:
            def p8_dbg1(sync):
                sync.wait_ge(sVE.h, state["S1"])
                sync.dma_start(out=d_hL1[:], in_=hL1[:]).then_inc(sDO.h, 16)
                sDO.inc(16)
            block.sync(p8_dbg1)
        if False:
            def p8_out1(sync):
                sync.wait_ge(sDO.h, sDO.v)
                sync.dma_start(out=y[0:128, 0:512], in_=hL1[:, 0:1024].bitcast(F32)) \
                    .then_inc(sDO.h, 16)
                sDO.inc(16)
                sync.wait_ge(sDO.h, sDO.v)
            block.sync(p8_out1)
            return nc, dbg_out

        ar.freebuf("xn1b", "w1sl80", "w1sl81", "w2sl80", "w2sl81", "t1s8", "t2s8")
        h1sl = [ar.alloc("h1sl0", [128, HC * 128], BF16),
                ar.alloc("h1sl1", [128, HC * 128], BF16)]
        g1st = [ar.alloc("g1st0", [128, T], BF16), ar.alloc("g1st1", [128, T], BF16)]

        s2_wld = {}
        s2_mm = {}
        s2_ev = {}
        s2_sp = {}

        def p8_s2_wld(g, oc):
            if oc == 0:
                g.wait_ge(sVE.h, state["S1"])
            if oc >= 2:
                g.wait_ge(sPE.h, s2_mm[oc - 2])
            g.dma_start(out=h1sl[oc % 2][:], in_=l1h1_t[oc]).then_inc(sWT.h, 16)
            s2_wld[oc] = sWT.inc(16)

        def p8_s2_mm(t, oc):
            t.wait_ge(sWT.h, s2_wld[oc])
            ps = psA if oc % 2 == 0 else psB
            if oc >= 2:
                t.wait_ge(sSE.h, s2_ev[oc - 2])
            for kc in range(HC):
                for nn_ in range(4):
                    ins = t.matmul(out=ps[:, nn_ * 512:(nn_ + 1) * 512],
                                   lhsT=h1sl[oc % 2][:, kc * 128:(kc + 1) * 128],
                                   rhs=hL1[:, kc * T + nn_ * 512: kc * T + (nn_ + 1) * 512],
                                   start=(kc == 0), stop=(kc == HC - 1))
            ins.then_inc(sPE.h, 1)
            s2_mm[oc] = sPE.inc()

        def p8_s2_ev(s, oc):
            ps = psA if oc % 2 == 0 else psB
            s.wait_ge(sPE.h, s2_mm[oc])
            if oc >= 2:
                s.wait_ge(sDO.h, s2_sp[oc - 2])
            s.activation(out=g1st[oc % 2][:], in_=ps[:, 0:T], func=AF.Gelu,
                         scale=l1s_sb[:, 2 * HC + oc:2 * HC + oc + 1]).then_inc(sSE.h, 1)
            s2_ev[oc] = sSE.inc()

        def p8_s2_sp(sync, oc):
            sync.wait_ge(sSE.h, s2_ev[oc])
            sync.dma_start(out=g1_hbm[oc], in_=g1st[oc % 2][:]).then_inc(sDO.h, 16)
            s2_sp[oc] = sDO.inc(16)

        for oc in range(HC):
            block.gpsimd(lambda g, oc=oc: p8_s2_wld(g, oc))
            block.tensor(lambda t, oc=oc: p8_s2_mm(t, oc))
            block.scalar(lambda s, oc=oc: p8_s2_ev(s, oc))
            block.sync(lambda sync, oc=oc: p8_s2_sp(sync, oc))
        state["S2"] = s2_sp[HC - 1]

        if False:
            def p8_out2(sync):
                sync.wait_ge(sDO.h, sDO.v)
                sync.dma_start(out=y[0:128, 0:D], in_=g1st[1][:, 0:T].bitcast(F32)) \
                    .then_inc(sDO.h, 16)
                sDO.inc(16)
                sync.wait_ge(sDO.h, sDO.v)
            block.sync(p8_out2)
            return nc, dbg_out

        ar.freebuf("hL1", "g1st0", "g1st1", "h1sl0", "h1sl1", "srow8")
        g2 = ar.alloc("g2", [128, HC * T], BF16)
        g1tt = ar.alloc("g1tt", [128, HC * 512], BF16)
        h2sl = [ar.alloc("h2sl0", [128, HC * 128], BF16),
                ar.alloc("h2sl1", [128, HC * 128], BF16)]

        s3_g1ld = {}
        s3_wld = {}
        s3_mm = {}
        s3_ev = {}

        def p8_s3_g1ld(sync, tt):
            if tt == 0:
                sync.wait_ge(sDO.h, state["S2"])
            else:
                sync.wait_ge(sSE.h, s3_ev[(tt - 1, HC - 1)])
            for kc in range(HC):
                sync.dma_start(out=g1tt[:, kc * 512:(kc + 1) * 512],
                               in_=g1_hbm[kc][:, tt * 512:(tt + 1) * 512]).then_inc(sIN.h, 16)
                sIN.inc(16)
            s3_g1ld[tt] = sIN.v

        def p8_s3_wld(g, tt, oc):
            k = tt * HC + oc
            if k == 0:
                g.wait_ge(sDO.h, state["S2"])
            if k >= 2:
                prev = (tt, oc - 2) if oc >= 2 else (tt - 1, HC - 2 + oc)
                g.wait_ge(sPE.h, s3_mm[prev])
            g.dma_start(out=h2sl[k % 2][:], in_=l1h2_t[oc]).then_inc(sWT.h, 16)
            s3_wld[(tt, oc)] = sWT.inc(16)

        def p8_s3_mm(t, tt, oc):
            k = tt * HC + oc
            slot = oc % 4
            t.wait_ge(sWT.h, s3_wld[(tt, oc)])
            t.wait_ge(sIN.h, s3_g1ld[tt])
            prev = (tt, oc - 4) if oc >= 4 else ((tt - 1, HC - 4 + oc) if tt >= 1 else None)
            if prev in s3_ev:
                t.wait_ge(sSE.h, s3_ev[prev])
            for kc in range(HC):
                ins = t.matmul(out=psA[:, slot * 512:(slot + 1) * 512],
                               lhsT=h2sl[k % 2][:, kc * 128:(kc + 1) * 128],
                               rhs=g1tt[:, kc * 512:(kc + 1) * 512],
                               start=(kc == 0), stop=(kc == HC - 1))
            ins.then_inc(sPE.h, 1)
            s3_mm[(tt, oc)] = sPE.inc()

        def p8_s3_ev(s, tt, oc):
            slot = oc % 4
            s.wait_ge(sPE.h, s3_mm[(tt, oc)])
            s.activation(out=g2[:, oc * T + tt * 512: oc * T + (tt + 1) * 512],
                         in_=psA[:, slot * 512:(slot + 1) * 512], func=AF.Gelu,
                         scale=l1s_sb[:, 3 * HC + oc:3 * HC + oc + 1]).then_inc(sSE.h, 1)
            s3_ev[(tt, oc)] = sSE.inc()

        for tt in range(4):
            block.sync(lambda sync, tt=tt: p8_s3_g1ld(sync, tt))
            for oc in range(HC):
                block.gpsimd(lambda g, tt=tt, oc=oc: p8_s3_wld(g, tt, oc))
                block.tensor(lambda t, tt=tt, oc=oc: p8_s3_mm(t, tt, oc))
                block.scalar(lambda s, tt=tt, oc=oc: p8_s3_ev(s, tt, oc))
        state["S3"] = s3_ev[(3, HC - 1)]

        ar.freebuf("g1tt", "h2sl0", "h2sl1", "stats_rp2")
        w3sB = ar.alloc("w3sB", [128, 1024], F32)
        ost = ar.alloc("ost", [128, 1024], F32)
        xtk = ar.alloc("xtk", [128, 1024], F32)
        w3b = ar.alloc("w3b", [128, HC * 1024], BF16)
        srow8b = ost

        def p8_s4_rp(g):
            g.wait_ge(sSE.h, state["S3"])
            g.dma_start(out=srow8b[0:1, 0:1024], in_=l1s_sb[:, 4 * HC:4 * HC + DC]) \
                .then_inc(sGC.h, 16)
            state["s4rp"] = sGC.inc(16)
        block.gpsimd(p8_s4_rp)

        def p8_s4_bc(t):
            t.wait_ge(sGC.h, state["s4rp"])
            t.wait_ge(sDO.h, state["S2"])
            for nn_ in range(2):
                ins = t.matmul(out=psB[:, nn_ * 512:(nn_ + 1) * 512], lhsT=ones_row[0:1, :],
                               rhs=srow8b[0:1, nn_ * 512:(nn_ + 1) * 512], start=True, stop=True)
            ins.then_inc(sPE.h, 1)
            state["s4bc"] = sPE.inc()
        block.tensor(p8_s4_bc)

        def p8_s4_bccp(v):
            v.wait_ge(sPE.h, state["s4bc"])
            v.tensor_copy(out=w3sB[:], in_=psB[:, 0:1024]).then_inc(sVE.h, 1)
            state["s4bcc"] = sVE.inc()
        block.vector(p8_s4_bccp)

        def p8_s4_wld(g):
            g.wait_ge(sPE.h, state["s4bc"])
            for kc in range(HC):
                g.dma_start(out=w3b[:, kc * 1024:(kc + 1) * 1024], in_=l1w3_t[kc]) \
                    .then_inc(sWT.h, 16)
                sWT.inc(16)
        block.gpsimd(p8_s4_wld)
        state["s4w"] = sWT.v

        d_w3sB = dbg_tensor("w3sB", [128, 1024])
        if d_w3sB is not None:
            def p8_dbg4(sync):
                sync.wait_ge(sVE.h, state["s4bcc"])
                sync.dma_start(out=d_w3sB[:], in_=w3sB[:]).then_inc(sDO.h, 16)
                sDO.inc(16)
            block.sync(p8_dbg4)

        s4_mm = {}
        s4_tt = {}
        s4_sp = {}
        s4_xld = {}

        def p8_s4_xld(sync, tt2):
            if tt2 == 0:
                sync.wait_ge(sSE.h, state["S3"])
            else:
                sync.wait_ge(sVE.h, s4_tt[tt2 - 1])
            sync.dma_start(out=xtk[:], in_=x1tok_hbm[tt2]).then_inc(sIN.h, 16)
            s4_xld[tt2] = sIN.inc(16)

        def p8_s4_mm(t, tt2, half):
            slot0 = (tt2 % 2) * 1024
            if tt2 == 0 and half == 0:
                t.wait_ge(sWT.h, state["s4w"])
                t.wait_ge(sVE.h, state["s4bcc"])
                t.wait_ge(sSE.h, state["S3"])
            if tt2 >= 2 and half == 0:
                t.wait_ge(sVE.h, s4_tt[tt2 - 2])
            for kc in range(HC):
                ins = t.matmul(out=psA[:, slot0 + half * 512: slot0 + (half + 1) * 512],
                               lhsT=g2[:, kc * T + tt2 * 128: kc * T + tt2 * 128 + 128],
                               rhs=w3b[:, kc * 1024 + half * 512: kc * 1024 + (half + 1) * 512],
                               start=(kc == 0), stop=(kc == HC - 1))
            ins.then_inc(sPE.h, 1)
            s4_mm[(tt2, half)] = sPE.inc()

        def p8_s4_tt(v, tt2):
            slot0 = (tt2 % 2) * 1024
            v.wait_ge(sPE.h, s4_mm[(tt2, 1)])
            v.wait_ge(sIN.h, s4_xld[tt2])
            if tt2 >= 1:
                v.wait_ge(sDO.h, s4_sp[tt2 - 1])
            v.tensor_tensor(out=ost[:], in0=psA[:, slot0:slot0 + 1024], in1=w3sB[:],
                            op=ALU.mult)
            v.drain()
            v.tensor_tensor(out=ost[:], in0=ost[:], in1=xtk[:],
                            op=ALU.add).then_inc(sVE.h, 1)
            s4_tt[tt2] = sVE.inc()

        def p8_s4_sp(sync, tt2):
            sync.wait_ge(sVE.h, s4_tt[tt2])
            sync.dma_start(out=y[tt2 * 128:(tt2 + 1) * 128, :], in_=ost[:]) \
                .then_inc(sDO.h, 16)
            s4_sp[tt2] = sDO.inc(16)

        for tt2 in range(16):
            block.sync(lambda sync, tt2=tt2: p8_s4_xld(sync, tt2))
            for half in range(2):
                block.tensor(lambda t, tt2=tt2, half=half: p8_s4_mm(t, tt2, half))
            block.vector(lambda v, tt2=tt2: p8_s4_tt(v, tt2))
            block.sync(lambda sync, tt2=tt2: p8_s4_sp(sync, tt2))

        def p8_fin(sync):
            sync.wait_ge(sDO.h, sDO.v)
        block.sync(p8_fin)

    return nc, dbg_out


def run(inputs, phases=99, dbg=(), cores=8):
    nc, dbg_out = build(phases, dbg)
    in_maps = [prep_inputs(inputs, c) for c in range(cores)]
    res = run_bass_kernel_spmd(nc, in_maps, core_ids=list(range(cores)))
    return res


def kernel(**inputs):
    nc, _ = build(99, ())
    in_maps = [prep_inputs(inputs, c) for c in range(8)]
    res = run_bass_kernel_spmd(nc, in_maps, core_ids=list(range(8)))
    out = np.empty((8, 2048, 1024), np.float32)
    for c in range(8):
        out[c] = np.asarray(res.results[c]["y"], np.float32)
    return out
